# version 13
# baseline (speedup 1.0000x reference)
import numpy as np
import ml_dtypes
import contextlib
import concourse.bass as bass
import concourse.mybir as mybir
from concourse.bass_utils import run_bass_kernel_spmd
from concourse.alu_op_type import AluOpType as Op

F32 = mybir.dt.float32
BF16 = mybir.dt.bfloat16
AF = mybir.ActivationFunctionType
BF = ml_dtypes.bfloat16
U8 = mybir.dt.uint8

B, L, D = 4, 8192, 1024
TPC = 4096
CH = 128
NCHUNK = TPC // CH
NTILE = TPC // 512
RC = 12582912.0

MODE = "fused"
L0_CUT = 99
ENG_ATTR = {"pe": "tensor", "act": "scalar", "dve": "vector", "pool": "gpsimd", "sp": "sync"}
ENGS = tuple(ENG_ATTR)


class Sched:
    def __init__(self, nc):
        self.nc = nc
        self.ops = []
        self.res = {}
        self.psum_keys = set()
        self.ns = ""
        self.barrier_deps = []

    def barrier(self):
        last = {}
        for i, o in enumerate(self.ops):
            if o["dma"] or o["cc"]:
                last[("d", o["eng"], o["wkeys"][0] if o["wkeys"] else i)] = i
            else:
                last[o["eng"]] = i
        self.barrier_deps = sorted(last.values())

    def op(self, eng, fn, r=(), w=(), dma=False, cc=False, relax=False):
        idx = len(self.ops)
        r = [self.ns + k for k in r]
        w = [self.ns + k for k in w]
        for k in list(r):
            if k in self.psum_keys:
                r.remove(k)
                if k not in w:
                    w.append(k)
        deps = set()
        for k in r:
            st = self.res.get(k)
            if st and st["w"] is not None:
                deps.add(st["w"])
        for k in w:
            st = self.res.get(k)
            if st:
                if st["w"] is not None:
                    deps.add(st["w"])
                deps.update(st["r"])
        for k in r:
            self.res.setdefault(k, {"w": None, "r": []})["r"].append(idx)
        for k in w:
            self.res[k] = {"w": idx, "r": []}
        deps.update(self.barrier_deps)
        deps.discard(idx)
        self.ops.append(dict(eng=eng, fn=fn, deps=sorted(deps), dma=dma, cc=cc,
                             wkeys=list(w), relax=relax))
        return idx

    def dma(self, eng, out, in_, r=(), w=()):
        self.op(eng, lambda e: e.dma_start(out=out, in_=in_), r=r, w=w, dma=True)

    def mm(self, out, lhsT, rhs, r, w, start=True, stop=True, tp=None, skip=False):
        self.op("pe", lambda e: e.matmul(out, lhsT, rhs, start=start, stop=stop,
                                         tile_position=tp, skip_group_check=skip),
                r=r, w=w, relax=not start)

    def trans(self, out, in_, ident, r, w):
        self.op("pe", lambda e: e.transpose(out, in_, ident), r=r, w=w)

    def cp(self, eng, out, in_, r, w):
        if eng == "act":
            self.op(eng, lambda e: e.copy(out, in_), r=r, w=w)
        else:
            self.op(eng, lambda e: e.tensor_copy(out, in_), r=r, w=w)

    def ts(self, eng, out, in0, s1, s2, op0, op1, r, w):
        self.op(eng, lambda e: e.tensor_scalar(out, in0, s1, s2, op0, op1), r=r, w=w)

    def tt(self, out, in0, in1, op, r, w):
        self.op("dve", lambda e: e.tensor_tensor(out, in0, in1, op), r=r, w=w)

    def stt(self, out, in0, sc, in1, op0, op1, r, w):
        self.op("dve", lambda e: e.scalar_tensor_tensor(out, in0, sc, in1, op0, op1), r=r, w=w)

    def act(self, out, in_, func, r, w, bias=None, scale=None, accum_out=None):
        kw = {}
        if bias is not None:
            kw["bias"] = bias
        if scale is not None:
            kw["scale"] = scale
        if accum_out is not None:
            kw["accum_out"] = accum_out
        self.op("act", lambda e: e.activation(out, in_, func, **kw), r=r, w=w)

    def red(self, out, in_, op, r, w, absv=None):
        self.op("dve", lambda e: e.tensor_reduce(out, in_, mybir.AxisListType.X, op,
                                                 apply_absolute_value=absv), r=r, w=w)

    def recip(self, out, in_, r, w):
        self.op("dve", lambda e: e.reciprocal(out, in_), r=r, w=w)

    def memset(self, eng, ap, val, w):
        self.op(eng, lambda e: e.memset(ap, val), r=(), w=w)

    def cpred(self, out, mask, data, r, w):
        self.op("dve", lambda e: e.copy_predicated(out, mask, data), r=r, w=w)

    def collective(self, ins, outs, r, w):
        self.op("pool", lambda e: e.collective_compute(
            "AllReduce", Op.add, replica_groups=[[0, 1], [2, 3], [4, 5], [6, 7]],
            ins=ins, outs=outs), r=r, w=w, cc=True)

    def finalize(self, es):
        nc = self.nc
        def dkey(i, o):
            wk = o["wkeys"][0] if o["wkeys"] else f"anon{i}"
            return ("dma", o["eng"], wk)
        sems = {}
        for e in ENGS:
            sems[(e, "c")] = es.enter_context(nc.semaphore(f"s_{e}_c"))
        dma_keys = []
        for i, o in enumerate(self.ops):
            if o["dma"]:
                k = dkey(i, o)
                if k not in sems:
                    nm = f"sd_{o['eng']}_{k[2]}".replace(":", "_")
                    sems[k] = es.enter_context(nc.semaphore(nm))
                    dma_keys.append(k)
        n_cc = sum(1 for o in self.ops if o["cc"])
        cc_sems = [es.enter_context(nc.semaphore(f"s_cc{i}")) for i in range(n_cc)]

        def is_async(d):
            return self.ops[d]["dma"] or self.ops[d]["cc"]

        for i, o in enumerate(self.ops):
            if not o.get("relax"):
                continue
            o["deps"] = [d for d in o["deps"]
                         if is_async(d) or self.ops[d]["eng"] != o["eng"]]
        needed = set()
        for o in self.ops:
            needed.update(o["deps"])
        tok, cnt, cc_i = {}, {}, 0
        for i, o in enumerate(self.ops):
            if o["cc"]:
                tok[i] = (("cc", cc_i), 1)
                cc_i += 1
            elif o["dma"]:
                key = dkey(i, o)
                cnt[key] = cnt.get(key, 0) + 16
                tok[i] = (key, cnt[key])
            elif i in needed:
                key = (o["eng"], "c")
                cnt[key] = cnt.get(key, 0) + 1
                tok[i] = (key, cnt[key])
        per_eng = {e: [] for e in ENGS}
        waited = {e: {} for e in ENGS}
        for i, o in enumerate(self.ops):
            waits = {}
            for d in o["deps"]:
                key, val = tok[d]
                if waits.get(key, 0) < val:
                    waits[key] = val
            todo = []
            wmap = waited[o["eng"]]
            for key, val in sorted(waits.items(), key=lambda kv: str(kv[0])):
                if wmap.get(key, 0) < val:
                    todo.append((key, val))
                    wmap[key] = val
            per_eng[o["eng"]].append((todo, o["fn"], tok.get(i)))

        fin = [(k, cnt[k]) for k in dma_keys]
        fin += [(("cc", i), 1) for i in range(n_cc)]
        wmap = waited["sp"]
        todo = [(k, v) for k, v in fin if wmap.get(k, 0) < v]
        per_eng["sp"].append((todo, lambda e: e.nop(), None))

        def sem_of(key):
            return cc_sems[key[1]] if key[0] == "cc" else sems[key]

        block = es.enter_context(nc.Block())

        def make_body(ops):
            def body(e):
                for waits, fn, inc in ops:
                    for key, val in waits:
                        e.wait_ge(sem_of(key), val)
                    inst = fn(e)
                    if inc is not None:
                        key, _ = inc
                        if key[0] == "cc":
                            inst.then_inc(sem_of(key))
                        else:
                            inst.then_inc(sem_of(key), 16 if key[0] == "dma" else 1)
            return body

        for e in ENGS:
            if per_eng[e]:
                getattr(block, ENG_ATTR[e])(make_body(per_eng[e]))


class TProxy:

    def __init__(self, h, name):
        self._h = h
        self.name = name

    def __getitem__(self, k):
        return self._h[k]

    def __getattr__(self, a):
        return getattr(self._h, a)


class Alloc:
    def __init__(self, nc, es, sched=None, prefix=""):
        self.nc, self.es, self.t = nc, es, {}
        self.sched = sched
        self.prefix = prefix

    def sb(self, name, shape, dt):
        h = self.es.enter_context(self.nc.sbuf_tensor(self.prefix + name, shape, dt))
        p = TProxy(h, name)
        self.t[name] = p
        return p

    def ps(self, name, shape, dt):
        h = self.es.enter_context(self.nc.psum_tensor(self.prefix + name, shape, dt))
        p = TProxy(h, name)
        self.t[name] = p
        if self.sched is not None:
            self.sched.psum_keys.add(self.sched.ns + name)
        return p


def host_constants(ins):
    c = {}
    c["ident"] = np.eye(128, dtype=BF)
    c["identf"] = np.eye(128, dtype=np.float32)
    for s, nm in ((16, "tri16"), (32, "tri32"), (64, "tri64"), (128, "tri128")):
        m = np.zeros((128, 128), np.float32)
        for t in range(128):
            m[(t // s) * s:t + 1, t] = 1.0
        c[nm] = m
    for s, nm in ((16, "rev16"), (32, "rev32"), (64, "rev64"), (128, "rev128")):
        m = np.zeros((128, 128), np.float32)
        for t in range(128):
            m[t + 1:(t // s) * s + s, t] = 1.0
        c[nm] = m
    c["amask"] = np.tile(np.triu(np.ones((128, 128), np.uint8)), (1, 8))
    c["onesrow"] = np.ones((1, 128), np.float32)
    c["cvals"] = np.tile(np.array([1e-5, 1.0, 1e-6, 0.0], np.float32)[None, :], (128, 1))
    c["onesb"] = np.ones((128, 1), dtype=BF)
    c["dtbb"] = np.tile(np.asarray(ins["L0_attn_dt_b"], np.float32)[None, :], (128, 1))
    c["bdb"] = np.tile(np.asarray(ins["L0_attn_base_decay"], np.float32).reshape(-1)[None, :], (128, 1))
    c["tfc"] = np.ascontiguousarray(
        np.asarray(ins["L0_attn_time_first"], np.float32).reshape(512).reshape(4, 128).T)
    c["wfTd"] = np.ascontiguousarray(
        np.vstack([ins["L1_attn_Wf"].T, ins["L1_attn_Wf"].T]).astype(np.float32))
    c["ebias"] = np.tile(np.asarray(ins["L1_ffn_e_bias"], np.float32)[None, :], (128, 1))
    ind = np.zeros((16, 1024), np.float32)
    for pair in range(8):
        for p in range(128):
            ind[2 * pair + p // 64, 128 * pair + p] = 1.0
    c["indn"] = ind.astype(BF)
    return c


def emit_l0(s, al, io):
    sb, ps = al.sb, al.ps

    ident = sb("ident", [128, 128], BF16)
    tri = {}
    for nm in ("tri16", "tri32", "tri64", "tri128", "rev16", "rev32", "rev64", "rev128"):
        tri[nm] = sb(nm, [128, 128], F32)
    amask = sb("amask", [128, 1024], U8)
    onesrow = sb("onesrow", [1, 128], F32)
    cvals = sb("cvals", [128, 4], F32)
    onesb = sb("onesb", [128, 1], BF16)
    dtbb = sb("dtbb", [128, 512], F32)
    bdb = sb("bdb", [128, 512], F32)
    tfc = sb("tfc", [128, 4], F32)
    sendm = sb("sendm", [128, 1], F32)
    wsel_a = sb("wsel_a", [128, 8, 128], BF16)
    wsel_b = sb("wsel_b", [128, 8, 32], BF16)
    dtw = sb("dtw", [32, 512], BF16)
    wk = sb("wk", [128, 8, 512], BF16)
    wv = sb("wv", [128, 8, 512], BF16)
    wr = sb("wr", [128, 8, 512], BF16)
    wg = sb("wg", [128, 8, 512], BF16)
    wo = sb("wo", [128, 4, 1024], BF16)
    qrhs = sb("qrhs", [128, 8, 1024], BF16)
    srhsb = sb("srhsb", [128, 1024], F32)
    sabs = sb("sabs", [128, 8], F32)
    smax = sb("smax", [128, 8], F32)
    srin = sb("srin", [128, 8], F32)
    srhs_c = sb("srhs_c", [128, 8], F32)

    stt_ = [sb(f"st{q}", [128, 64], F32) for q in range(4)]
    stb_ = [sb(f"stb{q}", [128, 64], BF16) for q in range(4)]
    stmp = sb("stmp", [128, 64], F32)
    scarry = sb("scarry", [128, 4], F32)
    pglob = sb("pglob", [128, 4], F32)
    stend = sb("stend", [128, 256], F32)

    xtm = sb("xtm", [128, 4, 1024], F32)
    sq32 = sb("sq32", [128, 1024], F32)
    q32 = [sb(f"q32_{i}", [128, 1024], F32) for i in range(2)]
    xnb = [sb(f"xnb{i}", [128, 1024], BF16) for i in range(2)]
    qtm = [sb(f"qtm{i}", [128, 1024], BF16) for i in range(2)]
    vec = {nm: sb(nm, [128, 4], F32) for nm in
           ("lsum", "lsq", "mean", "var", "rstd", "amax", "qrs", "slhs", "m2")}
    xnfm = sb("xnfm", [128, 8, 512], BF16)
    qfm = sb("qfm", [128, 8, 512], BF16)
    projfm_a = sb("projfm_a", [128, 512], BF16)
    projfm_b = sb("projfm_b", [32, 512], BF16)
    bsc = sb("bsc", [128, 4], F32)
    kfm = sb("kfm", [128, 4, 512], BF16)
    ktm = sb("ktm", [128, 4, 512], BF16)
    vtm = sb("vtm", [128, 4, 512], BF16)
    vhtm = sb("vhtm", [128, 4, 512], BF16)
    vfm = sb("vfm", [128, 4, 512], BF16)
    tfx = sb("tfx", [128, 4, 128], BF16)
    dtz = sb("dtz", [128, 512], F32)
    dt1 = sb("dt1", [128, 512], F32)
    rates = sb("rates", [128, 4, 512], F32)

    sf16 = sb("sf16", [128, 512], F32)
    sf128 = sb("sf128", [128, 512], F32)
    sc16 = sb("sc16", [128, 512], F32)
    c8t = sb("c8t", [128, 32], F32)
    ex1 = [sb(f"ex1_{i}", [128, 512], F32) for i in range(2)]
    tendt = sb("tendt", [128, 512], F32)
    khat = sb("khat", [128, 512], BF16)
    fac = {nm: sb("f_" + nm, [128, 512], BF16) for nm in
           ("qt", "aD", "bD", "a16", "b16", "a32", "b32", "a64", "b64")}
    qtg = sb("qtg", [128, 512], BF16)
    asb = sb("asb", [128, 1024], BF16)
    dcol = sb("dcol", [128, 4], F32)
    tfkv = sb("tfkv", [128, 4, 128], BF16)
    wkvtf = sb("wkvtf", [128, 4, 512], BF16)
    trr = sb("trr", [128, 512], F32)
    srgt = sb("srgt", [128, 4, 512], BF16)
    x1t = [sb("x1t0", [128, 1024], F32)]

    psa = ps("psa", [128, 512], F32)
    psb = ps("psb", [128, 512], F32)
    psc = ps("psc", [128, 512], F32)
    psA = ps("psA", [128, 1024], F32)
    psw = ps("psw", [128, 512], F32)
    pstA = ps("pstA", [128, 1024], BF16)
    pstB = ps("pstB", [128, 1024], BF16)

    for nm, buf in [("ident", ident), ("amask", amask), ("onesrow", onesrow),
                    ("cvals", cvals),
                    ("onesb", onesb), ("dtbb", dtbb), ("bdb", bdb), ("tfc", tfc),
                    ("send_mask", sendm)] + [(k, tri[k]) for k in tri]:
        s.dma("sp", buf[:], io[nm], w=[buf.name])
    wsel_r = io["Wsel"].rearrange("(c p) m -> p c m", p=128)
    s.dma("pool", wsel_a[:], wsel_r[:, :, 0:128], w=["wsel_a"])
    s.dma("pool", wsel_b[:], wsel_r[:, :, 128:160], w=["wsel_b"])
    s.dma("pool", dtw[:], io["dtW"], w=["dtw"])
    for nm, buf in (("Wk", wk), ("Wv", wv), ("Wr", wr), ("Wg", wg)):
        s.dma("pool", buf[:], io[nm].rearrange("(c p) m -> p c m", p=128), w=[buf.name])
    s.dma("pool", wo[:], io["Wo"].rearrange("(c p) m -> p c m", p=128), w=["wo"])

    if L0_CUT < 1:
        return
    rT = io["rhsT"].rearrange("(c p) m -> p c m", p=128)
    for half in range(2):
        xkey = "xtB"
        s.dma("sp", xtm[:], rT[:, 4 * half:4 * half + 4, :], w=["xtA", "xtB"])
        for j in range(4):
            jc = 4 * half + j
            s.red(sabs[:, jc:jc + 1], xtm[:, j, :], Op.max, r=[xkey],
                  w=["sabs"], absv=True)
        s.ts("dve", smax[:, 4 * half:4 * half + 4], sabs[:, 4 * half:4 * half + 4],
             1e-12, None, Op.max, Op.bypass, r=["sabs"], w=["smax"])
        s.recip(srin[:, 4 * half:4 * half + 4], smax[:, 4 * half:4 * half + 4],
                r=["smax"], w=["srin"])
        s.ts("dve", srhs_c[:, 4 * half:4 * half + 4], smax[:, 4 * half:4 * half + 4],
             1.0 / 127.0, None, Op.mult, Op.bypass, r=["smax"], w=["srhs_c"])
        for j in range(4):
            jc = 4 * half + j
            s.ts("dve", sq32[:], xtm[:, j, :], srin[:, jc:jc + 1], 127.0,
                 Op.mult, Op.mult, r=[xkey, "srin"], w=["sq32"])
            s.ts("dve", sq32[:], sq32[:], RC, RC, Op.add, Op.subtract, r=["sq32"], w=["sq32"])
            s.ts("dve", xnb[0][:], sq32[:], 127.0, -127.0, Op.min, Op.max,
                 r=["sq32"], w=["xnb0"])
            for dc in range(8):
                pt = pstA if dc % 2 == 0 else pstB
                s.trans(pt[:, 0:128], xnb[0][:, 128 * dc:128 * (dc + 1)],
                        ident[:], r=["xnb0", "ident"], w=[pt.name])
                s.cp("dve", qrhs[:, dc, 128 * jc:128 * (jc + 1)], pt[:, 0:128],
                     r=[pt.name], w=["qrhs"])
    for jc in range(8):
        s.dma("act", io["scr8"][jc], srhs_c[:, jc:jc + 1], r=["srhs_c"], w=["scr8d"])
    s.dma("sp", sq32[0:1, :], io["scr8"].rearrange("a b -> (a b)").unsqueeze(0),
          r=["scr8d"], w=["sq32"])
    for half in range(2):
        s.mm(psa[:, 0:512], onesrow[0:1, :], sq32[0:1, 512 * half:512 * (half + 1)],
             r=["sq32", "onesrow"], w=["psa"])
        s.cp("dve", srhsb[:, 512 * half:512 * (half + 1)], psa[:, 0:512],
             r=["psa"], w=["srhsb"])

    if L0_CUT < 2:
        return
    s.memset("dve", asb[:], 0.0, w=["asb"])
    s.memset("dve", psA[:], 0.0, w=["psA"])
    for q in range(4):
        s.memset("dve", stt_[q][:], 0.0, w=[f"st{q}"])
        s.memset("dve", stb_[q][:], 0.0, w=[f"stb{q}"])
    s.memset("dve", scarry[:], 0.0, w=["scarry"])

    x_dram, x1_dram = io["x"], io["x1"]
    qg_dram, srg_dram = io["qglob"], io["srg"]

    def xload(g, half):
        rows = slice(g * 512 + half * 256, g * 512 + half * 256 + 256)
        s.dma("sp", xtm[:, 2 * half:2 * half + 2, :],
              x_dram[rows, :].rearrange("(c p) d -> p c d", p=128),
              w=["xtA" if half == 0 else "xtB"])
    xload(0, 0)
    xload(0, 1)
    for g in range(NTILE):
        rows = slice(g * 512, (g + 1) * 512)
        for tc in range(4):
            xkey = "xtA" if tc < 2 else "xtB"
            xs = xtm[:, tc, :]
            s.red(vec["lsum"][:, tc:tc + 1], xs, Op.add, r=[xkey], w=["lsum"])
            s.tt(sq32[:], xs, xs, Op.mult, r=[xkey], w=["sq32"])
            s.red(vec["lsq"][:, tc:tc + 1], sq32[:], Op.add, r=["sq32"], w=["lsq"])
        s.ts("dve", vec["mean"][:], vec["lsum"][:], 1.0 / D, None, Op.mult, Op.bypass,
             r=["lsum"], w=["mean"])
        s.tt(vec["m2"][:], vec["mean"][:], vec["mean"][:], Op.mult, r=["mean"], w=["m2"])
        s.ts("dve", vec["var"][:], vec["lsq"][:], 1.0 / D, None, Op.mult, Op.bypass,
             r=["lsq"], w=["var"])
        s.tt(vec["var"][:], vec["var"][:], vec["m2"][:], Op.subtract, r=["var", "m2"], w=["var"])
        s.act(vec["rstd"][:], vec["var"][:], AF.Ln, r=["var", "cvals"], w=["rstd"], bias=cvals[:, 0:1])
        s.act(vec["rstd"][:], vec["rstd"][:], AF.Exp, r=["rstd"], w=["rstd"], scale=-0.5)
        for tc in range(4):
            sl = tc % 2
            kxn, kq, kqt = f"xnb{sl}", f"q32_{sl}", f"qtm{sl}"
            s.ts("dve", sq32[:], xtm[:, tc, :], vec["mean"][:, tc:tc + 1],
                 vec["rstd"][:, tc:tc + 1], Op.subtract, Op.mult,
                 r=["xtA" if tc < 2 else "xtB", "mean", "rstd"], w=["sq32"])
            s.cp("dve", xnb[sl][:], sq32[:], r=["sq32"], w=[kxn])
            s.red(vec["amax"][:, tc:tc + 1], sq32[:], Op.max, r=["sq32"], w=["amax"], absv=True)
            s.ts("dve", vec["amax"][:, tc:tc + 1], vec["amax"][:, tc:tc + 1], 1e-12, None,
                 Op.max, Op.bypass, r=["amax"], w=["amax"])
            s.recip(vec["qrs"][:, tc:tc + 1], vec["amax"][:, tc:tc + 1], r=["amax"], w=["qrs"])
            s.ts("dve", q32[sl][:], sq32[:], vec["qrs"][:, tc:tc + 1], None, Op.mult,
                 Op.bypass, r=["sq32", "qrs"], w=[kq])
            s.ts("pool", q32[sl][:], q32[sl][:], 127.0, RC, Op.mult, Op.add, r=[kq], w=[kq])
            s.ts("pool", q32[sl][:], q32[sl][:], -RC, 1.0, Op.add, Op.mult, r=[kq], w=[kq])
            s.ts("pool", qtm[sl][:], q32[sl][:], 127.0, -127.0, Op.min, Op.max,
                 r=[kq], w=[kqt])
            for dc in range(8):
                pt = pstA if dc % 2 == 0 else pstB
                s.trans(pt[:, 0:128], xnb[sl][:, 128 * dc:128 * (dc + 1)],
                        ident[:], r=[kxn, "ident"], w=[pt.name])
                s.cp("dve", xnfm[:, dc, 128 * tc:128 * (tc + 1)], pt[:, 0:128],
                     r=[pt.name], w=["xnfm"])
                pt2 = pstB if dc % 2 == 0 else pstA
                s.trans(pt2[:, 128:256], qtm[sl][:, 128 * dc:128 * (dc + 1)],
                        ident[:], r=[kqt, "ident"], w=[pt2.name])
                s.cp("dve", qfm[:, dc, 128 * tc:128 * (tc + 1)], pt2[:, 128:256],
                     r=[pt2.name], w=["qfm"])
        s.ts("dve", vec["slhs"][:], vec["amax"][:], 1.0 / 127.0, None, Op.mult, Op.bypass,
             r=["amax"], w=["slhs"])
        if L0_CUT < 3:
            continue
        for kc in range(8):
            s.mm(psa[:, 0:512], wsel_a[:, kc, :], xnfm[:, kc, :], r=["wsel_a", "xnfm"],
                 w=["psa"], start=(kc == 0), stop=(kc == 7))
            s.mm(psb[0:32, 0:512], wsel_b[:, kc, :], xnfm[:, kc, :], r=["wsel_b", "xnfm"],
                 w=["psb"], start=(kc == 0), stop=(kc == 7))
        s.cp("dve", projfm_a[:], psa[:, 0:512], r=["psa"], w=["projfm_a"])
        s.cp("dve", projfm_b[:], psb[0:32, 0:512], r=["psb"], w=["projfm_b"])
        if L0_CUT < 3.2:
            continue
        for tc in range(4):
            cols = slice(128 * tc, 128 * (tc + 1))
            s.mm(psc[0:128, 0:1], projfm_a[32:64, cols], onesb[32:64, :],
                 r=["projfm_a", "onesb"], w=["psc"], start=True, stop=False, tp=(32, 0))
            s.mm(psc[0:128, 0:1], projfm_a[64:96, cols], onesb[64:96, :],
                 r=["projfm_a", "onesb"], w=["psc"], start=False, stop=True, tp=(64, 0))
            s.act(bsc[:, tc:tc + 1], psc[:, 0:1], AF.Copy, r=["psc"], w=["bsc"],
                  scale=1.0 / 64.0)
        if L0_CUT < 3.4:
            continue
        for tc in range(4):
            cols = slice(128 * tc, 128 * (tc + 1))
            s.mm(psa[:, 0:512], projfm_a[0:32, cols], dtw[:], r=["projfm_a", "dtw"], w=["psa"])
            s.tt(dtz[:], psa[:, 0:512], dtbb[:], Op.add, r=["psa", "dtbb"], w=["dtz"])
            s.act(dt1[:], dtz[:], AF.Abs, r=["dtz"], w=["dt1"])
            s.act(dt1[:], dt1[:], AF.Exp, r=["dt1"], w=["dt1"], scale=-1.0)
            s.act(dt1[:], dt1[:], AF.Ln, r=["dt1", "cvals"], w=["dt1"], bias=cvals[:, 1:2])
            s.stt(dt1[:], dtz[:], 0.0, dt1[:], Op.max, Op.add, r=["dtz", "dt1"], w=["dt1"])
            s.tt(rates[:, tc, :], dt1[:], bdb[:], Op.mult, r=["dt1", "bdb"], w=["rates"])
        if L0_CUT < 3.45:
            continue
        for mc in range(4):
            pd = [psa, psb, psc, psa][mc]
            for kc in range(8):
                s.mm(pd[:, 0:512], wk[:, kc, 128 * mc:128 * (mc + 1)], xnfm[:, kc, :],
                     r=["wk", "xnfm"], w=[pd.name], start=(kc == 0), stop=(kc == 7))
            s.cp("dve", kfm[:, mc, :], pd[:, 0:512], r=[pd.name], w=["kfm"])
        if L0_CUT < 3.56:
            continue
        for tc in range(4):
            pd = [psa, psb, psc, psa][tc]
            for kc in range(8):
                s.mm(pd[:, 0:512], xnfm[:, kc, 128 * tc:128 * (tc + 1)], wv[:, kc, :],
                     r=["wv", "xnfm"], w=[pd.name], start=(kc == 0), stop=(kc == 7))
            s.cp("dve", vtm[:, tc, :], pd[:, 0:512], r=[pd.name], w=["vtm"])
            if L0_CUT < 3.62:
                continue
            s.ts("dve", vhtm[:, tc, :], pd[:, 0:512], bsc[:, tc:tc + 1], None,
                 Op.mult, Op.bypass, r=[pd.name, "bsc"], w=["vhtm"])
        for mc in range(4):
            for tc in range(4):
                pt = pstA if (tc % 2 == 0) else pstB

                s.trans(pt[:, 0:128], kfm[:, mc, 128 * tc:128 * (tc + 1)], ident[:],

                        r=["kfm", "ident"], w=[pt.name])

                s.cp("dve", ktm[:, tc, 128 * mc:128 * (mc + 1)], pt[:, 0:128],

                     r=[pt.name], w=["ktm"])
                pt = pstB if (tc % 2 == 0) else pstA

                s.trans(pt[:, 128:256], vtm[:, tc, 128 * mc:128 * (mc + 1)], ident[:],

                        r=["vtm", "ident"], w=[pt.name])

                s.cp("dve", vfm[:, mc, 128 * tc:128 * (tc + 1)], pt[:, 128:256],

                     r=[pt.name], w=["vfm"])

        if L0_CUT < 4:
            continue
        for cc in range(4):
            ci = g * 4 + cc
            rsl = rates[:, cc, :]
            ksl = kfm[:, :, 128 * cc:128 * (cc + 1)]

            def svar(nm, trik, pd, pkey, save=None):
                for q in range(4):
                    s.mm(pd[:, 128 * q:128 * (q + 1)], rsl[:, 128 * q:128 * (q + 1)],
                         tri[trik][:], r=["rates", trik], w=[pkey],
                         start=True, stop=True, skip=True)
                if save is not None:
                    s.cp("dve", save[0][:], pd[:, 0:512], r=[pkey], w=[save[1]])

            def facmake(nm, pd, pkey, scl, ei):
                if L0_CUT < 4.07:
                    return
                s.act(ex1[ei][:], pd[:, 0:512], AF.Exp, r=[pkey], w=[f"ex1_{ei}"], scale=scl)
                if L0_CUT < 4.09:
                    return
                s.op("dve", lambda e, nm=nm, ei=ei, ksl=ksl: e.tensor_tensor(
                    fac[nm][:].rearrange("p (a t) -> p a t", a=4), ksl,
                    ex1[ei][:].rearrange("p (a t) -> p a t", a=4), Op.mult),
                    r=["kfm", f"ex1_{ei}"], w=["f_" + nm])

            svar("S16", "tri16", psa, "psa", save=(sf16, "sf16"))
            facmake("a16", psa, "psa", -1.0, 0)
            svar("S32", "tri32", psb, "psb")
            facmake("a32", psb, "psb", -1.0, 1)
            svar("S64", "tri64", psa, "psa")
            facmake("a64", psa, "psa", -1.0, 0)
            svar("S128", "tri128", psb, "psb", save=(sf128, "sf128"))
            facmake("qt", psb, "psb", -1.0, 1)
            svar("R16", "rev16", psa, "psa")
            facmake("b16", psa, "psa", -1.0, 0)
            svar("R32", "rev32", psb, "psb")
            facmake("b32", psb, "psb", -1.0, 1)
            svar("R64", "rev64", psa, "psa")
            facmake("b64", psa, "psa", -1.0, 0)
            if L0_CUT < 4.2:
                continue
            s.ts("dve", c8t[:], sf16[:, 15::16], 0.5, None, Op.mult, Op.bypass,
                 r=["sf16"], w=["c8t"])
            s.op("dve", lambda e: e.tensor_tensor(
                sc16[:].rearrange("p (a c) -> p a c", a=32),
                sf16[:].rearrange("p (a c) -> p a c", a=32),
                c8t[:].unsqueeze(2).broadcast_to([128, 32, 16]),
                Op.subtract), r=["sf16", "c8t"], w=["sc16"])
            s.act(ex1[0][:], sc16[:], AF.Exp, r=["sc16"], w=["ex1_0"], scale=-1.0)
            s.op("dve", lambda e, ksl=ksl: e.tensor_tensor(
                fac["aD"][:].rearrange("p (a t) -> p a t", a=4), ksl,
                ex1[0][:].rearrange("p (a t) -> p a t", a=4), Op.mult),
                r=["kfm", "ex1_0"], w=["f_aD"])
            s.act(ex1[1][:], sc16[:], AF.Exp, r=["sc16"], w=["ex1_1"], scale=1.0)
            s.op("dve", lambda e, ksl=ksl: e.tensor_tensor(
                fac["bD"][:].rearrange("p (a t) -> p a t", a=4), ksl,
                ex1[1][:].rearrange("p (a t) -> p a t", a=4), Op.mult),
                r=["kfm", "ex1_1"], w=["f_bD"])
            if L0_CUT < 4.3:
                continue
            s.mm(psc[:, 0:512], tri["rev128"][:], rsl, r=["rates", "rev128"], w=["psc"])
            s.act(tendt[:], psc[:, 0:512], AF.Exp, r=["psc"], w=["tendt"], scale=-1.0)
            s.tt(khat[:], ktm[:, cc, :], tendt[:], Op.mult, r=["ktm", "tendt"], w=["khat"])
            if L0_CUT < 4.4:
                continue
            for h in range(8):
                q, po = h // 2, 64 * (h % 2)
                base = 128 * h

                def fs(f, lo, ln):
                    return fac[f][po:po + 64, 128 * q + lo:128 * q + lo + ln]
                for bq in range(4):
                    s.mm(psA[32 * bq:32 * bq + 32, base + 32 * bq:base + 32 * bq + 32],
                         fs("bD", 32 * bq, 32), fs("aD", 32 * bq, 32),
                         r=["f_bD", "f_aD"], w=["psA"], skip=True, tp=(po, 32 * bq))
                for bq in range(4):
                    s.mm(psA[32 * bq:32 * bq + 16, base + 32 * bq + 16:base + 32 * bq + 32],
                         fs("b16", 32 * bq, 16), fs("a16", 32 * bq + 16, 16),
                         r=["f_b16", "f_a16"], w=["psA"], skip=True, tp=(po, 32 * bq))
                for bq in range(2):
                    s.mm(psA[64 * bq:64 * bq + 32, base + 64 * bq + 32:base + 64 * bq + 64],
                         fs("b32", 64 * bq, 32), fs("a32", 64 * bq + 32, 32),
                         r=["f_b32", "f_a32"], w=["psA"], skip=True, tp=(po, 64 * bq))
                s.mm(psA[0:64, base + 64:base + 128], fs("b64", 0, 64), fs("a64", 64, 64),
                     r=["f_b64", "f_a64"], w=["psA"], skip=True, tp=(po, 0))
            if L0_CUT < 4.45:
                continue
            s.cpred(asb[:], amask[:], psA[:], r=["psA", "amask"], w=["asb"])
            if L0_CUT < 4.5:
                continue
            s.act(pglob[:], scarry[:], AF.Exp, r=["scarry"], w=["pglob"], scale=-1.0)
            s.op("dve", lambda e: e.tensor_tensor(
                qtg[:].rearrange("p (a t) -> p a t", a=4),
                fac["qt"][:].rearrange("p (a t) -> p a t", a=4),
                pglob[:].unsqueeze(2).broadcast_to([128, 4, 128]),
                Op.mult), r=["f_qt", "pglob"], w=["qtg"])
            s.dma("act", qg_dram[ci], qtg[:], r=["qtg"], w=["qgd"])
            s.tt(scarry[:], scarry[:], sf128[:, 127::128], Op.add,
                 r=["scarry", "sf128"], w=["scarry"])
            s.act(dcol[:], sf128[:, 127::128], AF.Exp, r=["sf128"], w=["dcol"], scale=-1.0)
            if L0_CUT < 4.6:
                continue
            for h in range(8):
                q, po = h // 2, 64 * (h % 2)
                dst = psw[po:po + 64, 128 * q:128 * (q + 1)]
                s.mm(dst, stb_[q][po:po + 64, :], fac["qt"][po:po + 64, 128 * q:128 * (q + 1)],
                     r=[f"stb{q}", "f_qt"], w=["psw"], start=True, stop=False, skip=True,
                     tp=(po, po))
                s.mm(dst, vhtm[:, cc, 64 * h:64 * h + 64], asb[:, 128 * h:128 * (h + 1)],
                     r=["vhtm", "asb"], w=["psw"], start=False, stop=True, skip=True,
                     tp=(0, po))
            s.op("dve", lambda e, cc=cc: e.tensor_tensor(
                tfx[:], kfm[:, :, 128 * cc:128 * (cc + 1)],
                tfc[:].unsqueeze(2).broadcast_to([128, 4, 128]), Op.mult),
                r=["kfm", "tfc"], w=["tfx"])
            s.op("dve", lambda e, cc=cc: e.tensor_tensor(
                tfkv[:], tfx[:], vfm[:, :, 128 * cc:128 * (cc + 1)], Op.mult),
                r=["tfx", "vfm"], w=["tfkv"])
            s.op("dve", lambda e, cc=cc: e.tensor_tensor(
                wkvtf[:, :, 128 * cc:128 * (cc + 1)],
                psw[:].rearrange("p (a t) -> p a t", a=4), tfkv[:], Op.add),
                r=["psw", "tfkv"], w=["wkvtf"])
            if L0_CUT < 4.8:
                continue
            for h in range(8):
                q, po = h // 2, 64 * (h % 2)
                s.mm(psc[po:po + 64, 64 * q:64 * q + 64], khat[:, 64 * h:64 * h + 64],
                     vhtm[:, cc, 64 * h:64 * h + 64], r=["khat", "vhtm"], w=["psc"],
                     skip=True, tp=(0, po))
            for q in range(4):
                s.ts("dve", stmp[:], stt_[q][:], dcol[:, q:q + 1], None, Op.mult, Op.bypass,
                     r=[f"st{q}", "dcol"], w=["stmp"])
                s.tt(stt_[q][:], stmp[:], psc[:, 64 * q:64 * q + 64], Op.add,
                     r=["stmp", "psc"], w=[f"st{q}"])
                s.cp("dve", stb_[q][:], stt_[q][:], r=[f"st{q}"], w=[f"stb{q}"])

        if L0_CUT < 5:
            continue
        for mc in range(4):
            pd = [psa, psb][mc % 2]
            for kc in range(8):
                s.mm(pd[:, 0:512], wr[:, kc, 128 * mc:128 * (mc + 1)], xnfm[:, kc, :],
                     r=["wr", "xnfm"], w=[pd.name], start=(kc == 0), stop=(kc == 7))
            s.act(trr[:], pd[:, 0:512], AF.Tanh, r=[pd.name], w=["trr"], scale=0.5)
            pd2 = psc
            for kc in range(8):
                s.mm(pd2[:, 0:512], wg[:, kc, 128 * mc:128 * (mc + 1)], xnfm[:, kc, :],
                     r=["wg", "xnfm"], w=[pd2.name], start=(kc == 0), stop=(kc == 7))
            s.act(dtz[:], pd2[:, 0:512], AF.Tanh, r=[pd2.name], w=["dtz"], scale=0.5)
            s.ts("dve", dt1[:], trr[:], 1.0, None, Op.add, Op.bypass, r=["trr"], w=["dt1"])
            s.stt(srgt[:, mc, :], dtz[:], 1.0, dt1[:], Op.add, Op.mult,
                  r=["dtz", "dt1"], w=["srgt"])
            s.stt(wkvtf[:, mc, :], srgt[:, mc, :], 0.25, wkvtf[:, mc, :], Op.mult, Op.mult,
                  r=["srgt", "wkvtf"], w=["wkvtf"])
        s.dma("act", srg_dram[g], srgt[:], r=["srgt"], w=["srgd"])
        for tc in range(4):
            sl = 0
            cols = slice(128 * tc, 128 * (tc + 1))
            for half in range(2):
                pa_, pf_ = (psa, psb) if half == 0 else (psc, psa)
                hc = slice(512 * half, 512 * (half + 1))
                for kc in range(4):
                    s.mm(pa_[:, 0:512], wkvtf[:, kc, cols], wo[:, kc, 512 * half:512 * (half + 1)],
                         r=["wkvtf", "wo"], w=[pa_.name], start=(kc == 0), stop=(kc == 3))
                for kc in range(8):
                    s.mm(pf_[:, 0:512], qfm[:, kc, cols], qrhs[:, kc, 512 * half:512 * (half + 1)],
                         r=["qfm", "qrhs"], w=[pf_.name], start=(kc == 0), stop=(kc == 7))
                s.tt(x1t[sl][:, hc], xtm[:, tc, hc], pa_[:, 0:512], Op.add,
                     r=["xtA" if tc < 2 else "xtB", pa_.name], w=[f"x1t{sl}"])
                s.stt(trr[:], pf_[:, 0:512], vec["slhs"][:, tc:tc + 1], srhsb[:, hc],
                      Op.mult, Op.mult, r=[pf_.name, "slhs", "srhsb"], w=["trr"])
                s.tt(x1t[sl][:, hc], x1t[sl][:, hc], trr[:], Op.add,
                     r=[f"x1t{sl}", "trr"], w=[f"x1t{sl}"])
            rr = slice(g * 512 + tc * 128, g * 512 + tc * 128 + 128)
            s.dma("act", x1_dram[rr, :], x1t[sl][:], r=[f"x1t{sl}"], w=["x1d"])
            if g + 1 < NTILE and tc in (1, 3):
                xload(g + 1, 0 if tc == 1 else 1)

    if L0_CUT < 6:
        return
    for q in range(4):
        s.ts("dve", stend[:, 64 * q:64 * (q + 1)], stt_[q][:], sendm[:, 0:1], None,
             Op.mult, Op.bypass, r=[f"st{q}", "sendm"], w=["stend"])
    s.dma("pool", io["cc_in"], stend[:], r=["stend"], w=["ccin"])
    s.collective([io["cc_in"].opt()], [io["cc_out"].opt()], r=["ccin"], w=["ccout"])
    s.dma("pool", stend[:], io["cc_out"], r=["ccout"], w=["stend"])
    s.dma("pool", io["st_mid"], stend[:], r=["stend"], w=["stmidd"])


def emit_l0c(s, al, io):
    sb, ps = al.sb, al.ps
    wo = sb("wo_c", [128, 4, 1024], BF16)
    stm = sb("stm", [128, 256], F32)
    stmb = sb("stmb", [128, 256], BF16)
    usem = sb("usem", [128, 1], F32)
    qtg_ = [sb(f"qtg_c{i}", [128, 4, 512], BF16) for i in range(2)]
    srgt_ = [sb(f"srgt_c{i}", [128, 4, 512], BF16) for i in range(2)]
    dys = sb("dys", [128, 4, 512], BF16)
    x1t_ = [sb(f"x1t_c{i}", [128, 4, 1024], F32) for i in range(2)]
    psw = ps("psw_c", [128, 512], F32)
    psa = ps("psa_c", [128, 512], F32)
    psb = ps("psb_c", [128, 512], F32)

    s.dma("pool", wo[:], io["Wo"].rearrange("(c p) m -> p c m", p=128), w=["wo_c"])
    s.dma("sp", stm[:], io["st_mid_in"], w=["stm_raw"])
    s.dma("sp", usem[:], io["use_mask"], w=["usem"])
    s.ts("dve", stmb[:], stm[:], usem[:, 0:1], None, Op.mult, Op.bypass,
         r=["stm_raw", "usem"], w=["stmb"])

    def l0c_load(g):
        p = g % 2
        rows = slice(g * 512, (g + 1) * 512)
        s.dma("sp", x1t_[p][:], io["x1_in"][rows, :].rearrange("(c p) d -> p c d", p=128),
              w=[f"x1t_c{p}"])
        s.dma("sp", srgt_[p][:], io["srg_in"][g], w=[f"srgt_c{p}"])
        for cc in range(4):
            s.dma("sp", qtg_[p][:, cc, :], io["qglob_in"][g * 4 + cc], w=[f"qtg_c{p}"])
    l0c_load(0)
    for g in range(NTILE):
        rows = slice(g * 512, (g + 1) * 512)
        p = g % 2
        x1t, srgt, qtg = x1t_[p], srgt_[p], qtg_[p]
        xk, sk_, qk = f"x1t_c{p}", f"srgt_c{p}", f"qtg_c{p}"
        if g + 1 < NTILE:
            l0c_load(g + 1)
        for cc in range(4):
            for h in range(8):
                q, po = h // 2, 64 * (h % 2)
                s.mm(psw[po:po + 64, 128 * q:128 * (q + 1)],
                     stmb[po:po + 64, 64 * q:64 * q + 64],
                     qtg[po:po + 64, cc, 128 * q:128 * (q + 1)],
                     r=["stmb", qk], w=["psw_c"], skip=True, tp=(po, po))
            s.op("dve", lambda e, cc=cc: e.scalar_tensor_tensor(
                dys[:, :, 128 * cc:128 * (cc + 1)],
                srgt[:, :, 128 * cc:128 * (cc + 1)], 0.25,
                psw[:].rearrange("p (a t) -> p a t", a=4),
                Op.mult, Op.mult), r=[sk_, "psw_c"], w=["dys"])
        for tc in range(4):
            cols = slice(128 * tc, 128 * (tc + 1))
            for half, pd in ((0, psa), (1, psb)):
                for kc in range(4):
                    s.mm(pd[:, 0:512], dys[:, kc, cols], wo[:, kc, 512 * half:512 * (half + 1)],
                         r=["dys", "wo_c"], w=[pd.name], start=(kc == 0), stop=(kc == 3))
                s.tt(x1t[:, tc, 512 * half:512 * (half + 1)],
                     x1t[:, tc, 512 * half:512 * (half + 1)], pd[:, 0:512], Op.add,
                     r=[xk, pd.name], w=[xk])
        s.dma("act", io["x1c"][rows, :].rearrange("(c p) d -> p c d", p=128), x1t[:],
              r=[xk], w=["x1cd"])



def emit_ln(s, al, pref, xc_ap, xckey, xnfm_ap, xnfmkey, n_tc, psA_, psAkey, ident):
    t = al.t
    sq = t[pref + "sq"]
    xnb = t[pref + "xnb"]
    v_ = {nm: t[pref + nm] for nm in ("lsum", "lsq", "mean", "var", "rstd", "m2")}
    cvals = t["cvals"]
    for tc in range(n_tc):
        xs = xc_ap[:, tc, :]
        s.red(v_["lsum"][:, tc:tc + 1], xs, Op.add, r=[xckey], w=[pref + "lsum"])
        s.tt(sq[:], xs, xs, Op.mult, r=[xckey], w=[pref + "sq"])
        s.red(v_["lsq"][:, tc:tc + 1], sq[:], Op.add, r=[pref + "sq"], w=[pref + "lsq"])
    nt = slice(0, n_tc)
    s.ts("dve", v_["mean"][:, nt], v_["lsum"][:, nt], 1.0 / D, None, Op.mult, Op.bypass,
         r=[pref + "lsum"], w=[pref + "mean"])
    s.tt(v_["m2"][:, nt], v_["mean"][:, nt], v_["mean"][:, nt], Op.mult,
         r=[pref + "mean"], w=[pref + "m2"])
    s.ts("dve", v_["var"][:, nt], v_["lsq"][:, nt], 1.0 / D, None, Op.mult, Op.bypass,
         r=[pref + "lsq"], w=[pref + "var"])
    s.tt(v_["var"][:, nt], v_["var"][:, nt], v_["m2"][:, nt], Op.subtract,
         r=[pref + "var", pref + "m2"], w=[pref + "var"])
    s.act(v_["rstd"][:, nt], v_["var"][:, nt], AF.Ln, r=[pref + "var", "cvals"],
          w=[pref + "rstd"], bias=cvals[:, 0:1])
    s.act(v_["rstd"][:, nt], v_["rstd"][:, nt], AF.Exp, r=[pref + "rstd"],
          w=[pref + "rstd"], scale=-0.5)
    for tc in range(n_tc):
        s.ts("dve", sq[:], xc_ap[:, tc, :], v_["mean"][:, tc:tc + 1],
             v_["rstd"][:, tc:tc + 1], Op.subtract, Op.mult,
             r=[xckey, pref + "mean", pref + "rstd"], w=[pref + "sq"])
        s.cp("dve", xnb[:], sq[:], r=[pref + "sq"], w=[pref + "xnb"])
        for dc in range(8):
            s.trans(psA_[:, 0:128], xnb[:, 128 * dc:128 * (dc + 1)], ident[:],
                    r=[pref + "xnb", "ident"], w=[psAkey])
            s.cp("dve", xnfm_ap[:, dc, 128 * tc:128 * (tc + 1)], psA_[:, 0:128],
                 r=[psAkey], w=[xnfmkey])


def _ln_scratch(al, pref):
    al.sb(pref + "sq", [128, 1024], F32)
    al.sb(pref + "xnb", [128, 1024], BF16)
    for nm in ("lsum", "lsq", "mean", "var", "rstd", "m2"):
        al.sb(pref + nm, [128, 8], F32)


def emit_l1kv(s, al, io):
    sb, ps = al.sb, al.ps
    ident = sb("ident", [128, 128], BF16)
    cvals = sb("cvals", [128, 4], F32)
    onesb = sb("onesb", [128, 1], BF16)
    wkT = sb("wkT", [128, 8, 1024], BF16)
    wvT = sb("wvT", [128, 8, 1024], BF16)
    wfd = sb("wfd", [128, 256], BF16)
    xc_ = [sb(f"xc{i}", [128, 4, 1024], F32) for i in range(2)]
    xnfm = sb("xnfm", [128, 8, 512], BF16)
    kfm1 = sb("kfm1", [128, 8, 512], BF16)
    vtm1 = sb("vtm1", [128, 4, 1024], BF16)
    exs_ = [sb(f"exs{i}", [128, 512], F32) for i in range(2)]
    kft_ = [sb(f"kft{i}", [128, 512], BF16) for i in range(2)]
    kva = sb("kva", [128, 2, 1024], F32)
    ksa = sb("ksa", [128, 32], F32)
    _ln_scratch(al, "ln_")
    psa = ps("psa", [128, 512], F32)
    psb = ps("psb", [128, 512], F32)
    kvp0 = ps("kvp0", [128, 1024], F32)
    kvp1 = ps("kvp1", [128, 1024], F32)
    ksp = ps("ksp", [128, 512], F32)
    pstA = ps("pstA", [128, 1024], BF16)

    s.dma("sp", ident[:], io["ident"], w=["ident"])
    s.dma("sp", cvals[:], io["cvals"], w=["cvals"])
    s.dma("sp", onesb[:], io["onesb"], w=["onesb"])
    s.dma("pool", wkT[:], io["l1wkT"].rearrange("(c p) m -> p c m", p=128), w=["wkT"])
    s.dma("pool", wvT[:], io["l1wvT"].rearrange("(c p) m -> p c m", p=128), w=["wvT"])
    s.dma("pool", wfd[:], io["wfTd"], w=["wfd"])

    s.memset("dve", kva[:], 0.0, w=["kva"])
    s.memset("dve", ksa[:], 0.0, w=["ksa"])
    s.dma("sp", xc_[0][:], io["x1c_in"][0:512, :].rearrange("(c p) d -> p c d", p=128),
          w=["xc0"])
    for g in range(NTILE):
        p = g % 2
        xc, xck = xc_[p], f"xc{p}"
        if g + 1 < NTILE:
            nrows = slice((g + 1) * 512, (g + 2) * 512)
            s.dma("sp", xc_[(g + 1) % 2][:],
                  io["x1c_in"][nrows, :].rearrange("(c p) d -> p c d", p=128),
                  w=[f"xc{(g + 1) % 2}"])
        emit_ln(s, al, "ln_", xc[:], xck, xnfm[:], "xnfm", 4, pstA, "pstA", ident)
        for mc in range(8):
            pd = [psa, psb][mc % 2]
            for kc in range(8):
                s.mm(pd[:, 0:512], wkT[:, kc, 128 * mc:128 * (mc + 1)], xnfm[:, kc, :],
                     r=["wkT", "xnfm"], w=[pd.name], start=(kc == 0), stop=(kc == 7))
            s.cp("dve", kfm1[:, mc, :], pd[:, 0:512], r=[pd.name], w=["kfm1"])
        for tc in range(4):
            for half in range(2):
                pd = [psa, psb][half]
                for kc in range(8):
                    s.mm(pd[:, 0:512], xnfm[:, kc, 128 * tc:128 * (tc + 1)],
                         wvT[:, kc, 512 * half:512 * (half + 1)],
                         r=["wvT", "xnfm"], w=[pd.name], start=(kc == 0), stop=(kc == 7))
                s.cp("dve", vtm1[:, tc, 512 * half:512 * (half + 1)], pd[:, 0:512],
                     r=[pd.name], w=["vtm1"])
        for tc in range(4):
            for pr in range(8):
                pd = [psa, psb][pr % 2]
                for par in range(2):
                    h = 2 * pr + par
                    po = 64 * par
                    s.mm(pd[:, 256 * par:256 * par + 256],
                         kfm1[po:po + 64, pr, 128 * tc:128 * (tc + 1)],
                         wfd[po:po + 64, :],
                         r=["kfm1", "wfd"], w=[pd.name], tp=(po, 0), skip=True)
                pp = pr % 2
                exs, exsk = exs_[pp], f"exs{pp}"
                kft, kftk = kft_[pp], f"kft{pp}"
                s.ts("dve", kft[:], pd[:, 0:512], 0.0, None, Op.max, Op.bypass,
                     r=[pd.name], w=[kftk])
                s.act(exs[:], pd[:, 0:512], AF.Exp, r=[pd.name], w=[exsk])
                s.stt(kft[:], exs[:], 1.0, kft[:], Op.min, Op.add,
                      r=[exsk], w=[kftk])
                for par in range(2):
                    h = 2 * pr + par
                    for fc in range(2):
                        kvp = [kvp0, kvp1][fc]
                        s.mm(kvp[:, 64 * h:64 * h + 64],
                             kft[:, 256 * par + 128 * fc:256 * par + 128 * fc + 128],
                             vtm1[:, tc, 64 * h:64 * h + 64],
                             r=[kftk, "vtm1"], w=[kvp.name], skip=True)
                    for fc in range(2):
                        s.mm(ksp[:, 2 * h + fc:2 * h + fc + 1],
                             kft[:, 256 * par + 128 * fc:256 * par + 128 * fc + 128],
                             onesb[:, :],
                             r=[kftk, "onesb"], w=["ksp"], skip=True)
            for fc, kvp in ((0, kvp0), (1, kvp1)):
                s.tt(kva[:, fc, :], kva[:, fc, :], kvp[:, 0:1024], Op.add,
                     r=["kva", kvp.name], w=["kva"])
            s.tt(ksa[:], ksa[:], ksp[:, 0:32], Op.add, r=["ksa", "ksp"], w=["ksa"])
    s.dma("pool", io["cc_in"][:, 0:1024], kva[:, 0, :], r=["kva"], w=["ccin_a"])
    s.dma("pool", io["cc_in"][:, 1024:2048], kva[:, 1, :], r=["kva"], w=["ccin_b"])
    s.dma("pool", io["cc_in"][:, 2048:2080], ksa[:], r=["ksa"], w=["ccin_c"])
    s.collective([io["cc_in"].opt()], [io["cc_out"].opt()],
                 r=["ccin_a", "ccin_b", "ccin_c"], w=["ccout"])
    s.dma("pool", kva[:, 0, :], io["cc_out"][:, 0:1024], r=["ccout"], w=["kva"])
    s.dma("pool", kva[:, 1, :], io["cc_out"][:, 1024:2048], r=["ccout"], w=["kva"])
    s.dma("pool", ksa[:], io["cc_out"][:, 2048:2080], r=["ccout"], w=["ksa"])
    s.dma("act", io["kvf"][0], kva[:, 0, :], r=["kva"], w=["kvfd"])
    s.dma("act", io["kvf"][1], kva[:, 1, :], r=["kva"], w=["kvfd2"])
    s.dma("act", io["ksumf"], ksa[:], r=["ksa"], w=["ksumfd"])


def emit_l1out(s, al, io):
    sb, ps = al.sb, al.ps
    ident = sb("ident", [128, 128], BF16)
    identf = sb("identf", [128, 128], F32)
    cvals = sb("cvals", [128, 4], F32)
    ebias = sb("ebias", [128, 16], F32)
    indn = sb("indn", [16, 1024], BF16)
    woT = sb("woT", [128, 8, 1024], BF16)
    wfd = sb("wfd", [128, 256], BF16)
    rtT = sb("rtT", [128, 8, 16], BF16)
    kvsb = sb("kvsb", [128, 2, 1024], BF16)
    kss = sb("kss", [128, 32], F32)
    knd = sb("knd", [128, 32, 16], BF16)
    xc = sb("xc", [128, 8, 1024], F32)
    xnfm = sb("xnfm", [128, 8, 1024], BF16)
    qfm1 = sb("qfm1", [128, 8, 512], BF16)
    qff_ = [sb(f"qff{i}", [128, 2, 512], BF16) for i in range(2)]
    exs_ = [sb(f"exs{i}", [128, 1024], BF16) for i in range(2)]
    nrc = sb("nrc", [16, 1024], BF16)
    rbb = sb("rbb", [128, 512], BF16)
    ofm = sb("ofm", [128, 8, 512], BF16)
    es1 = sb("es1", [16, 512], F32)
    sco = sb("sco", [16, 1024], F32)
    sctm = sb("sctm", [128, 8, 16], F32)
    scb = sb("scb", [128, 8, 16], F32)
    tmpc = sb("tmpc", [128, 8, 16], F32)
    cnt4 = sb("cnt4", [128, 8, 4], F32)
    gsum = sb("gsum", [128, 8, 4], F32)
    cntg = sb("cntg", [128, 8, 4], F32)
    gmask = sb("gmask", [128, 8, 4], F32)
    msc = sb("msc", [128, 8, 16], F32)
    cnt16 = sb("cnt16", [128, 8, 16], F32)
    sel4 = sb("sel4", [128, 8, 16], F32)
    cmb = sb("cmb", [128, 8, 16], F32)
    tsum = sb("tsum", [128, 8], F32)
    cmbF = sb("cmbF", [16, 1024], F32)
    cbr = sb("cbr", [128, 1024], BF16)
    sg = [sb(f"sg{i}", [128, 512], BF16) for i in range(2)]
    uc = [sb(f"uc{i}", [128, 512], BF16) for i in range(2)]
    hst = [sb(f"hst{u}", [128, 4, 1024], BF16) for u in range(3)]
    wga = [sb(f"wga{i}", [128, 8, 512], BF16) for i in range(2)]
    wua = [sb(f"wua{i}", [128, 8, 512], BF16) for i in range(2)]
    wda = [sb(f"wda{i}", [128, 4, 1024], BF16) for i in range(3)]
    _ln_scratch(al, "ln_")
    pg0 = ps("pg0", [128, 1024], F32)
    pg1 = ps("pg1", [128, 1024], F32)
    pya = ps("pya", [128, 1024], F32)
    pnn = ps("pnn", [16, 512], F32)
    pstA = ps("pstA", [128, 1024], BF16)

    s.dma("sp", ident[:], io["ident"], w=["ident"])
    s.dma("sp", identf[:], io["identf"], w=["identf"])
    s.dma("sp", cvals[:], io["cvals"], w=["cvals"])
    s.dma("sp", ebias[:], io["ebias"], w=["ebias"])
    s.dma("sp", indn[:], io["indn"], w=["indn"])
    s.dma("pool", woT[:], io["l1woT"].rearrange("(c p) m -> p c m", p=128), w=["woT"])
    s.dma("pool", wfd[:], io["wfTd"], w=["wfd"])
    s.dma("pool", rtT[:], io["routerT"].rearrange("(c p) m -> p c m", p=128), w=["rtT"])
    s.dma("pool", kvsb[:], io["kvf_in"].rearrange("a p m -> p a m"), w=["kvsb"])
    s.dma("sp", kss[:], io["ksumf_in"], w=["kss"])
    s.memset("dve", knd[:], 0.0, w=["knd"])
    for j in range(32):
        s.cp("dve", knd[:, j, j // 2:j // 2 + 1], kss[:, j:j + 1], r=["kss"], w=["knd"])

    GROUPS = [(0, 3), (3, 6), (6, 9), (9, 12), (12, 15), (15, 18)]
    for m in range(4):
        rows = slice(m * 1024, (m + 1) * 1024)
        s.dma("sp", xc[:], io["x1c_in"][rows, :].rearrange("(c p) d -> p c d", p=128),
              w=["xc"])
        wqr = io["l1wqT"].rearrange("(c p) m -> p c m", p=128)
        s.dma("pool", wga[0][:], wqr[:, :, 0:512], w=["wga0"])
        s.dma("pool", wga[1][:], wqr[:, :, 512:1024], w=["wga1"])
        emit_ln(s, al, "ln_", xc[:], "xc", xnfm[:], "xnfm", 8, pstA, "pstA", ident)

        for sub in range(2):
            tsl = slice(512 * sub, 512 * (sub + 1))
            for mc in range(8):
                pgx = [pg0, pg1][mc % 2]
                wqb = wga[mc // 4]
                for kc in range(8):
                    s.mm(pgx[:, 0:512], wqb[:, kc, 128 * (mc % 4):128 * (mc % 4 + 1)],
                         xnfm[:, kc, tsl], r=[wqb.name, "xnfm"], w=[pgx.name],
                         start=(kc == 0), stop=(kc == 7))
                s.cp("dve", qfm1[:, mc, :], pgx[:, 0:512], r=[pgx.name], w=["qfm1"])
            nfirst = True
            for pr in range(8):
                for par in range(2):
                    h = 2 * pr + par
                    po = 64 * par
                    pgx = [pg0, pg1][h % 2]
                    for fc in range(2):
                        s.mm(pgx[:, 512 * fc:512 * fc + 512],
                             wfd[po:po + 64, 128 * fc:128 * (fc + 1)],
                             qfm1[po:po + 64, pr, :],
                             r=["wfd", "qfm1"], w=[pgx.name], tp=(po, 0), skip=True)
                    pp = h % 2
                    exs, exsk = exs_[pp], f"exs{pp}"
                    qff, qffk = qff_[pp], f"qff{pp}"
                    qfl = qff[:].rearrange("p a t -> p (a t)")
                    s.ts("dve", qfl, pgx[:, 0:1024], 0.0, None, Op.max,
                         Op.bypass, r=[pgx.name], w=[qffk])
                    s.act(exs[:], pgx[:, 0:1024], AF.Exp, r=[pgx.name], w=[exsk])
                    s.op("dve", lambda e, qfl=qfl, exs=exs: e.scalar_tensor_tensor(
                        qfl, exs[:], 1.0, qfl,
                        Op.min, Op.add), r=[exsk], w=[qffk])
                    for fc in range(2):
                        s.mm(pya[po:po + 64, 0:512], kvsb[:, fc, 64 * h:64 * h + 64],
                             qff[:, fc, :], r=["kvsb", qffk], w=["pya"],
                             start=(fc == 0), stop=(fc == 1), skip=True, tp=(0, po))
                        s.mm(pnn[0:16, 0:512], knd[:, 2 * h + fc, :], qff[:, fc, :],
                             r=["knd", qffk], w=["pnn"], start=nfirst, stop=False,
                             skip=True)
                        nfirst = False
                s.cp("act", ofm[:, pr, :], pya[:, 0:512], r=["pya"], w=["ofm"])
            s.act(es1[:], pnn[0:16, 0:512], AF.Ln, r=["pnn", "cvals"], w=["es1"],
                  bias=cvals[0:16, 2:3])
            s.act(nrc[:, tsl], es1[:], AF.Exp, r=["es1"], w=["nrc"], scale=-1.0)
            for pr in range(8):
                s.mm(pya[:, 0:512], indn[:, 128 * pr:128 * (pr + 1)], nrc[:, tsl],
                     r=["indn", "nrc"], w=["pya"])
                s.cp("act", rbb[:], pya[:, 0:512], r=["pya"], w=["rbb"])
                s.tt(ofm[:, pr, :], ofm[:, pr, :], rbb[:], Op.mult,
                     r=["ofm", "rbb"], w=["ofm"])
            for tc in range(4):
                mc = sub * 4 + tc
                for half in range(2):
                    hs = slice(512 * half, 512 * (half + 1))
                    for pr in range(8):
                        s.mm(pya[:, hs], ofm[:, pr, 128 * tc:128 * (tc + 1)],
                             woT[:, pr, hs], r=["ofm", "woT"], w=["pya"],
                             start=(pr == 0), stop=(pr == 7))
                s.tt(xc[:, mc, :], xc[:, mc, :], pya[:, 0:1024], Op.add,
                     r=["xc", "pya"], w=["xc"])

        for half in range(2):
            for kc in range(8):
                s.mm(pnn[0:16, 0:512], rtT[:, kc, :], xnfm[:, kc, 512 * half:512 * (half + 1)],
                     r=["rtT", "xnfm"], w=["pnn"], start=(kc == 0), stop=(kc == 7))
            s.act(es1[:], pnn[0:16, 0:512], AF.Exp, r=["pnn"], w=["es1"], scale=-1.0)
            s.ts("dve", es1[:], es1[:], 1.0, None, Op.add, Op.bypass, r=["es1"], w=["es1"])
            s.recip(sco[:, 512 * half:512 * (half + 1)], es1[:], r=["es1"], w=["sco"])
        for tc in range(8):
            s.trans(pya[:, 0:16], sco[0:16, 128 * tc:128 * (tc + 1)], identf[0:16, 0:16],
                    r=["sco", "identf"], w=["pya"])
            s.cp("dve", sctm[:, tc, :], pya[:, 0:16], r=["pya"], w=["sctm"])
        s.op("dve", lambda e: e.tensor_tensor(
            scb[:], sctm[:], ebias[:].unsqueeze(1).broadcast_to([128, 8, 16]), Op.add),
            r=["sctm", "ebias"], w=["scb"])

        def ranks(src_ap, srckey, cnt_ap, cntkey, n, viewfn, tmpfn):
            s.memset("dve", cnt_ap, 0.0, w=[cntkey])
            for d in range(1, n):
                L = viewfn(src_ap, 0, n - d)
                Rv = viewfn(src_ap, d, n - d)
                cL = viewfn(cnt_ap, 0, n - d)
                cR = viewfn(cnt_ap, d, n - d)
                tv = tmpfn(n - d)
                s.tt(tv, L, Rv, Op.is_ge, r=[srckey], w=["tmpc"])
                s.tt(cR, cR, tv, Op.add, r=[cntkey, "tmpc"], w=[cntkey])
                s.tt(tv, Rv, L, Op.is_gt, r=[srckey], w=["tmpc"])
                s.tt(cL, cL, tv, Op.add, r=[cntkey, "tmpc"], w=[cntkey])

        def v16(ap, off, ln):
            return ap[:, :, off:off + ln]

        def v4(ap, off, ln):
            return ap.rearrange("p a (b c) -> p (a b) c", c=4)[:, :, off:off + ln]

        def v4g(ap, off, ln):
            return ap[:, :, off:off + ln]

        ranks(scb[:], "scb", cnt16[:], "cnt16", 4, v4,
              lambda ln: tmpc[:].rearrange("p a (b c) -> p (a b) c", c=4)[:, :, 0:ln])
        s.ts("dve", cnt16[:], cnt16[:], 2.0, None, Op.is_lt, Op.bypass,
             r=["cnt16"], w=["cnt16"])
        s.tt(msc[:], scb[:], cnt16[:], Op.mult, r=["scb", "cnt16"], w=["msc"])
        s.op("dve", lambda e: e.tensor_reduce(
            gsum[:], msc[:].rearrange("p a (b c) -> p (a b) c", c=4),
            mybir.AxisListType.X, Op.add), r=["msc"], w=["gsum"])
        ranks(gsum[:], "gsum", cntg[:], "cntg", 4, v4g,
              lambda ln: tmpc[:, :, 0:ln])
        s.ts("dve", gmask[:], cntg[:], 2.0, None, Op.is_lt, Op.bypass,
             r=["cntg"], w=["gmask"])
        s.op("dve", lambda e: e.tensor_tensor(
            msc[:].rearrange("p a (b c) -> p a b c", c=4),
            scb[:].rearrange("p a (b c) -> p a b c", c=4),
            gmask[:].unsqueeze(3).broadcast_to([128, 8, 4, 4]), Op.mult),
            r=["scb", "gmask"], w=["msc"])
        ranks(msc[:], "msc", cnt16[:], "cnt16", 16, v16,
              lambda ln: tmpc[:, :, 0:ln])
        s.ts("dve", sel4[:], cnt16[:], 4.0, None, Op.is_lt, Op.bypass,
             r=["cnt16"], w=["sel4"])
        s.tt(cmb[:], sctm[:], sel4[:], Op.mult, r=["sctm", "sel4"], w=["cmb"])
        s.op("dve", lambda e: e.tensor_reduce(
            tsum[:], cmb[:], mybir.AxisListType.X, Op.add), r=["cmb"], w=["tsum"])
        s.ts("dve", tsum[:], tsum[:], 1e-20, None, Op.add, Op.bypass,
             r=["tsum"], w=["tsum"])
        s.recip(tsum[:], tsum[:], r=["tsum"], w=["tsum"])
        for mc in range(8):
            s.ts("dve", cmb[:, mc, :], cmb[:, mc, :], tsum[:, mc:mc + 1], 2.5,
                 Op.mult, Op.mult, r=["cmb", "tsum"], w=["cmb"])
        for tc in range(8):
            s.trans(pya[0:16, 0:128], cmb[:, tc, :], identf[:],
                    r=["cmb", "identf"], w=["pya"])
            s.cp("dve", cmbF[:, 128 * tc:128 * (tc + 1)], pya[0:16, 0:128],
                 r=["pya"], w=["cmbF"])
        s.dma("act", io["cmbd"], cmbF[:], r=["cmbF"], w=["cmbd"])

        for gi, (e0, e1) in enumerate(GROUPS):
            for e in range(e0, e1):
                u = e - e0
                gb, ub, db = wga[e % 2], wua[e % 2], wda[u]
                s.dma("pool", gb[:], io["gateT"][e].rearrange("(c p) m -> p c m", p=128),
                      w=[gb.name])
                s.dma("pool", ub[:], io["upT"][e].rearrange("(c p) m -> p c m", p=128),
                      w=[ub.name])
                s.dma("pool", db[:], io["downT"][e].rearrange("(c p) m -> p c m", p=128),
                      w=[db.name])
                if e < 16:
                    s.dma("pool", cbr[:], io["cmbd"][e].unsqueeze(0).broadcast_to([128, 1024]),
                          r=["cmbd"], w=["cbr"])
                for ic in range(4):
                    for sub in range(2):
                        tsl = slice(512 * sub, 512 * (sub + 1))
                        pp = (ic * 2 + sub) % 2
                        pgx = [pg0, pg1][pp]
                        for kc in range(8):
                            s.mm(pgx[:, 0:512], gb[:, kc, 128 * ic:128 * (ic + 1)],
                                 xnfm[:, kc, tsl], r=[gb.name, "xnfm"], w=[pgx.name],
                                 start=(kc == 0), stop=(kc == 7), skip=True)
                            s.mm(pgx[:, 512:1024], ub[:, kc, 128 * ic:128 * (ic + 1)],
                                 xnfm[:, kc, tsl], r=[ub.name, "xnfm"], w=[pgx.name],
                                 start=(kc == 0), stop=(kc == 7), skip=True)
                        s.act(sg[pp][:], pgx[:, 0:512], AF.Silu, r=[pgx.name], w=[f"sg{pp}"])
                        if e < 16:
                            s.tt(uc[pp][:], pgx[:, 512:1024], cbr[:, tsl], Op.mult,
                                 r=[pgx.name, "cbr"], w=[f"uc{pp}"])
                        else:
                            s.cp("dve", uc[pp][:], pgx[:, 512:1024], r=[pgx.name], w=[f"uc{pp}"])
                        s.tt(hst[u][:, ic, tsl], sg[pp][:], uc[pp][:], Op.mult,
                             r=[f"sg{pp}", f"uc{pp}"], w=[f"hst{u}"])
            for tc in range(8):
                nu = e1 - e0
                for half in range(2):
                    hs = slice(512 * half, 512 * (half + 1))
                    kk = 0
                    for u in range(nu):
                        for ic in range(4):
                            s.mm(pya[:, hs], hst[u][:, ic, 128 * tc:128 * (tc + 1)],
                                 wda[u][:, ic, hs], r=[f"hst{u}", wda[u].name],
                                 w=["pya"], start=(kk == 0), stop=(kk == 4 * nu - 1))
                            kk += 1
                s.tt(xc[:, tc, :], xc[:, tc, :], pya[:, 0:1024], Op.add,
                     r=["xc", "pya"], w=["xc"])
        s.dma("act", io["outp"][rows, :].rearrange("(c p) d -> p c d", p=128), xc[:],
              r=["xc"], w=["outd"])


def _dram(nc, name, shape, dt, kind):
    return nc.dram_tensor("d_" + name, shape, dt, kind=kind).ap()


def build_l0():
    nc = bass.Bass("TRN2", target_bir_lowering=False)
    io = {"x": _dram(nc, "x", [TPC, D], F32, "ExternalInput")}
    for nm, shp in (("Wsel", [D, 160]), ("dtW", [32, 512]), ("Wk", [D, 512]),
                    ("Wv", [D, 512]), ("Wr", [D, 512]), ("Wg", [D, 512]),
                    ("Wo", [512, D]), ("rhsT", [D, D])):
        io[nm] = _dram(nc, nm, shp, F32, "ExternalInput")
    for nm, shp, dt in (("ident", [128, 128], BF16), ("amask", [128, 1024], U8),
                        ("onesrow", [1, 128], F32), ("onesb", [128, 1], BF16),
                        ("cvals", [128, 4], F32),
                        ("dtbb", [128, 512], F32), ("bdb", [128, 512], F32),
                        ("tfc", [128, 4], F32), ("send_mask", [128, 1], F32)):
        io[nm] = _dram(nc, nm, shp, dt, "ExternalInput")
    for nm in ("tri16", "tri32", "tri64", "tri128", "rev16", "rev32", "rev64", "rev128"):
        io[nm] = _dram(nc, nm, [128, 128], F32, "ExternalInput")
    io["x1"] = _dram(nc, "x1", [TPC, D], F32, "ExternalOutput")
    io["qglob"] = _dram(nc, "qglob", [NCHUNK, 128, 512], BF16, "ExternalOutput")
    io["srg"] = _dram(nc, "srg", [NTILE, 128, 4, 512], BF16, "ExternalOutput")
    io["st_mid"] = _dram(nc, "st_mid", [128, 256], F32, "ExternalOutput")
    io["scr8"] = _dram(nc, "scr8", [8, 128], F32, "Internal")
    io["cc_in"] = nc.dram_tensor("d_cc_in", [128, 256], F32).ap()
    io["cc_out"] = nc.dram_tensor("d_cc_out", [128, 256], F32).ap()
    es = contextlib.ExitStack()
    with es:
        sch = Sched(nc)
        al = Alloc(nc, es, sch)
        emit_l0(sch, al, io)
        sch.finalize(es)
    return nc


def build_l0c():
    nc = bass.Bass("TRN2", target_bir_lowering=False)
    io = {
        "x1_in": _dram(nc, "x1_in", [TPC, D], F32, "ExternalInput"),
        "qglob_in": _dram(nc, "qglob_in", [NCHUNK, 128, 512], BF16, "ExternalInput"),
        "srg_in": _dram(nc, "srg_in", [NTILE, 128, 4, 512], BF16, "ExternalInput"),
        "st_mid_in": _dram(nc, "st_mid_in", [128, 256], F32, "ExternalInput"),
        "use_mask": _dram(nc, "use_mask", [128, 1], F32, "ExternalInput"),
        "Wo": _dram(nc, "Wo", [512, D], F32, "ExternalInput"),
        "x1c": _dram(nc, "x1c", [TPC, D], F32, "ExternalOutput"),
    }
    es = contextlib.ExitStack()
    with es:
        sch = Sched(nc)
        al = Alloc(nc, es, sch)
        emit_l0c(sch, al, io)
        sch.finalize(es)
    return nc


def build_l1kv():
    nc = bass.Bass("TRN2", target_bir_lowering=False)
    io = {
        "x1c_in": _dram(nc, "x1c_in", [TPC, D], F32, "ExternalInput"),
        "l1wkT": _dram(nc, "l1wkT", [D, D], F32, "ExternalInput"),
        "l1wvT": _dram(nc, "l1wvT", [D, D], F32, "ExternalInput"),
        "wfTd": _dram(nc, "wfTd", [128, 256], F32, "ExternalInput"),
        "ident": _dram(nc, "ident", [128, 128], BF16, "ExternalInput"),
        "cvals": _dram(nc, "cvals", [128, 4], F32, "ExternalInput"),
        "onesb": _dram(nc, "onesb", [128, 1], BF16, "ExternalInput"),
        "kvf": _dram(nc, "kvf", [2, 128, 1024], F32, "ExternalOutput"),
        "ksumf": _dram(nc, "ksumf", [128, 32], F32, "ExternalOutput"),
    }
    io["cc_in"] = nc.dram_tensor("d_cc_in", [128, 2080], F32).ap()
    io["cc_out"] = nc.dram_tensor("d_cc_out", [128, 2080], F32).ap()
    es = contextlib.ExitStack()
    with es:
        sch = Sched(nc)
        al = Alloc(nc, es, sch)
        emit_l1kv(sch, al, io)
        sch.finalize(es)
    return nc


def build_l1out():
    nc = bass.Bass("TRN2", target_bir_lowering=False)
    io = {
        "x1c_in": _dram(nc, "x1c_in", [TPC, D], F32, "ExternalInput"),
        "kvf_in": _dram(nc, "kvf_in", [2, 128, 1024], F32, "ExternalInput"),
        "ksumf_in": _dram(nc, "ksumf_in", [128, 32], F32, "ExternalInput"),
        "l1wqT": _dram(nc, "l1wqT", [D, D], F32, "ExternalInput"),
        "l1woT": _dram(nc, "l1woT", [D, D], F32, "ExternalInput"),
        "wfTd": _dram(nc, "wfTd", [128, 256], F32, "ExternalInput"),
        "routerT": _dram(nc, "routerT", [D, 16], F32, "ExternalInput"),
        "ebias": _dram(nc, "ebias", [128, 16], F32, "ExternalInput"),
        "indn": _dram(nc, "indn", [16, 1024], BF16, "ExternalInput"),
        "ident": _dram(nc, "ident", [128, 128], BF16, "ExternalInput"),
        "identf": _dram(nc, "identf", [128, 128], F32, "ExternalInput"),
        "cvals": _dram(nc, "cvals", [128, 4], F32, "ExternalInput"),
        "gateT": _dram(nc, "gateT", [18, D, 512], F32, "ExternalInput"),
        "upT": _dram(nc, "upT", [18, D, 512], F32, "ExternalInput"),
        "downT": _dram(nc, "downT", [18, 512, D], F32, "ExternalInput"),
        "cmbd": _dram(nc, "cmbd", [16, 1024], F32, "Internal"),
        "outp": _dram(nc, "outp", [TPC, D], F32, "ExternalOutput"),
    }
    es = contextlib.ExitStack()
    with es:
        sch = Sched(nc)
        al = Alloc(nc, es, sch)
        emit_l1out(sch, al, io)
        sch.finalize(es)
    return nc


def build_fused():
    nc = bass.Bass("TRN2", target_bir_lowering=False)
    io = {}
    io["x"] = _dram(nc, "x", [TPC, D], F32, "ExternalInput")
    for nm, shp in (("Wsel", [D, 160]), ("dtW", [32, 512]), ("Wk", [D, 512]),
                    ("Wv", [D, 512]), ("Wr", [D, 512]), ("Wg", [D, 512]),
                    ("Wo", [512, D]), ("rhsT", [D, D]),
                    ("l1wkT", [D, D]), ("l1wvT", [D, D]), ("l1wqT", [D, D]),
                    ("l1woT", [D, D]), ("routerT", [D, 16]),
                    ("gateT", [18, D, 512]), ("upT", [18, D, 512]),
                    ("downT", [18, 512, D]), ("wfTd", [128, 256])):
        io[nm] = _dram(nc, nm, shp, F32, "ExternalInput")
    for nm, shp, dt in (("ident", [128, 128], BF16), ("identf", [128, 128], F32),
                        ("amask", [128, 1024], U8),
                        ("onesrow", [1, 128], F32), ("onesb", [128, 1], BF16),
                        ("cvals", [128, 4], F32),
                        ("dtbb", [128, 512], F32), ("bdb", [128, 512], F32),
                        ("tfc", [128, 4], F32), ("send_mask", [128, 1], F32),
                        ("use_mask", [128, 1], F32), ("ebias", [128, 16], F32),
                        ("indn", [16, 1024], BF16)):
        io[nm] = _dram(nc, nm, shp, dt, "ExternalInput")
    for nm in ("tri16", "tri32", "tri64", "tri128", "rev16", "rev32", "rev64", "rev128"):
        io[nm] = _dram(nc, nm, [128, 128], F32, "ExternalInput")
    io["outp"] = _dram(nc, "outp", [TPC, D], F32, "ExternalOutput")
    for nm, shp, dt in (("x1", [TPC, D], F32), ("qglob", [NCHUNK, 128, 512], BF16),
                        ("srg", [NTILE, 128, 4, 512], BF16), ("st_mid", [128, 256], F32),
                        ("scr8", [8, 128], F32), ("x1c", [TPC, D], F32),
                        ("kvf", [2, 128, 1024], F32), ("ksumf", [128, 32], F32),
                        ("cmbd", [16, 1024], F32)):
        io[nm] = _dram(nc, nm, shp, dt, "Internal")
    io["cc_st_in"] = nc.dram_tensor("d_cc_st_in", [128, 256], F32).ap()
    io["cc_st_out"] = nc.dram_tensor("d_cc_st_out", [128, 256], F32).ap()
    io["cc_kv_in"] = nc.dram_tensor("d_cc_kv_in", [128, 2080], F32).ap()
    io["cc_kv_out"] = nc.dram_tensor("d_cc_kv_out", [128, 2080], F32).ap()

    io0 = dict(io)
    io0["cc_in"], io0["cc_out"] = io["cc_st_in"], io["cc_st_out"]
    io0c = dict(io, x1_in=io["x1"], qglob_in=io["qglob"], srg_in=io["srg"],
                st_mid_in=io["st_mid"])
    io1k = dict(io, x1c_in=io["x1c"])
    io1k["cc_in"], io1k["cc_out"] = io["cc_kv_in"], io["cc_kv_out"]
    io1o = dict(io, x1c_in=io["x1c"], kvf_in=io["kvf"], ksumf_in=io["ksumf"])

    es_outer = contextlib.ExitStack()
    with es_outer:
        sch = Sched(nc)
        phases = ((emit_l0, io0, "p0_"), (emit_l0c, io0c, "p1_"),
                  (emit_l1kv, io1k, "p2_"), (emit_l1out, io1o, "p3_"))
        for emit, iom, pref in phases:
            with contextlib.ExitStack() as es_p:
                al = Alloc(nc, es_p, sch, prefix=pref)
                sch.ns = pref
                emit(sch, al, iom)
            sch.barrier()
        sch.ns = ""
        sch.finalize(es_outer)
    return nc


def kernel_fused(ins, consts, wprep):
    x = ins["L0_attn_x"]
    nc = build_fused()
    base = {
        "Wsel": ins["L0_attn_W_sel"], "dtW": ins["L0_attn_dt_W"],
        "Wk": ins["L0_attn_Wk"], "Wv": ins["L0_attn_Wv"], "Wr": ins["L0_attn_Wr"],
        "Wg": ins["L0_attn_Wg"], "Wo": ins["L0_attn_Wo"],
        "rhsT": np.ascontiguousarray(ins["L0_ffn_rhs"].T),
    }
    for nm in ("l1wqT", "l1wkT", "l1wvT", "l1woT", "routerT", "gateT", "upT", "downT"):
        base[nm] = wprep[nm]
    for nm in ("ident", "identf", "amask", "onesrow", "onesb", "cvals", "dtbb",
               "bdb", "tfc", "ebias", "indn", "wfTd",
               "tri16", "tri32", "tri64", "tri128", "rev16", "rev32", "rev64", "rev128"):
        base[nm] = consts[nm]
    in_maps = []
    for c in range(8):
        m = dict(base)
        b, h = c // 2, c % 2
        m["x"] = np.ascontiguousarray(x[b, h * TPC:(h + 1) * TPC, :])
        m["send_mask"] = np.full((128, 1), 1.0 if h == 0 else 0.0, np.float32)
        m["use_mask"] = np.full((128, 1), 0.0 if h == 0 else 1.0, np.float32)
        in_maps.append({"d_" + k: v for k, v in m.items()})
    res = run_bass_kernel_spmd(nc, in_maps, core_ids=list(range(8)))
    out = np.zeros((B, L, D), np.float32)
    for c in range(8):
        b, h = c // 2, c % 2
        out[b, h * TPC:(h + 1) * TPC, :] = res.results[c]["d_outp"]
    return out


def _prep_weights(ins):
    w = {}
    w["l1wqT"] = np.ascontiguousarray(ins["L1_attn_Wq"].T)
    w["l1wkT"] = np.ascontiguousarray(ins["L1_attn_Wk"].T)
    w["l1wvT"] = np.ascontiguousarray(ins["L1_attn_Wv"].T)
    w["l1woT"] = np.ascontiguousarray(ins["L1_attn_Wo"].T)
    w["routerT"] = np.ascontiguousarray(ins["L1_ffn_router_weight"].T)
    gT = ins["L1_ffn_gate_w"].transpose(0, 2, 1)
    shg = ins["L1_ffn_sh_gate_w"].T.reshape(D, 2, 512).transpose(1, 0, 2)
    w["gateT"] = np.ascontiguousarray(np.concatenate([gT, shg], 0))
    uT = ins["L1_ffn_up_w"].transpose(0, 2, 1)
    shu = ins["L1_ffn_sh_up_w"].T.reshape(D, 2, 512).transpose(1, 0, 2)
    w["upT"] = np.ascontiguousarray(np.concatenate([uT, shu], 0))
    dT = ins["L1_ffn_down_w"].transpose(0, 2, 1)
    shd = ins["L1_ffn_sh_down_w"].T.reshape(2, 512, D)
    w["downT"] = np.ascontiguousarray(np.concatenate([dT, shd], 0))
    return w


def kernel(**inputs):
    ins = {k: np.asarray(v) for k, v in inputs.items()}
    consts = host_constants(ins)
    wprep = _prep_weights(ins)
    if MODE == "fused":
        return kernel_fused(ins, consts, wprep)
    x = ins["L0_attn_x"]

    nc0 = build_l0()
    base0 = {
        "Wsel": ins["L0_attn_W_sel"], "dtW": ins["L0_attn_dt_W"],
        "Wk": ins["L0_attn_Wk"], "Wv": ins["L0_attn_Wv"], "Wr": ins["L0_attn_Wr"],
        "Wg": ins["L0_attn_Wg"], "Wo": ins["L0_attn_Wo"],
        "rhsT": np.ascontiguousarray(ins["L0_ffn_rhs"].T),
    }
    for nm in ("ident", "amask", "onesrow", "onesb", "cvals", "dtbb", "bdb", "tfc",
               "tri16", "tri32", "tri64", "tri128", "rev16", "rev32", "rev64", "rev128"):
        base0[nm] = consts[nm]
    in_maps = []
    for c in range(8):
        m = dict(base0)
        b, h = c // 2, c % 2
        m["x"] = np.ascontiguousarray(x[b, h * TPC:(h + 1) * TPC, :])
        m["send_mask"] = np.full((128, 1), 1.0 if h == 0 else 0.0, np.float32)
        in_maps.append({"d_" + k: v for k, v in m.items()})
    res0 = run_bass_kernel_spmd(nc0, in_maps, core_ids=list(range(8)))

    nc1 = build_l0c()
    in_maps1 = []
    for c in range(8):
        r = res0.results[c]
        in_maps1.append({"d_" + k: v for k, v in {
            "x1_in": r["d_x1"], "qglob_in": r["d_qglob"], "srg_in": r["d_srg"],
            "st_mid_in": r["d_st_mid"],
            "use_mask": np.full((128, 1), 0.0 if c % 2 == 0 else 1.0, np.float32),
            "Wo": ins["L0_attn_Wo"],
        }.items()})
    res1 = run_bass_kernel_spmd(nc1, in_maps1, core_ids=list(range(8)))

    nc2 = build_l1kv()
    in_maps2 = []
    for c in range(8):
        in_maps2.append({"d_" + k: v for k, v in {
            "x1c_in": res1.results[c]["d_x1c"],
            "l1wkT": wprep["l1wkT"], "l1wvT": wprep["l1wvT"],
            "wfTd": consts["wfTd"], "ident": consts["ident"],
            "cvals": consts["cvals"], "onesb": consts["onesb"],
        }.items()})
    res2 = run_bass_kernel_spmd(nc2, in_maps2, core_ids=list(range(8)))

    nc3 = build_l1out()
    in_maps3 = []
    for c in range(8):
        in_maps3.append({"d_" + k: v for k, v in {
            "x1c_in": res1.results[c]["d_x1c"],
            "kvf_in": res2.results[c]["d_kvf"], "ksumf_in": res2.results[c]["d_ksumf"],
            "l1wqT": wprep["l1wqT"], "l1woT": wprep["l1woT"],
            "wfTd": consts["wfTd"], "routerT": wprep["routerT"],
            "ebias": consts["ebias"], "indn": consts["indn"],
            "ident": consts["ident"], "identf": consts["identf"],
            "cvals": consts["cvals"],
            "gateT": wprep["gateT"], "upT": wprep["upT"], "downT": wprep["downT"],
        }.items()})
    res3 = run_bass_kernel_spmd(nc3, in_maps3, core_ids=list(range(8)))

    out = np.zeros((B, L, D), np.float32)
    for c in range(8):
        b, h = c // 2, c % 2
        out[b, h * TPC:(h + 1) * TPC, :] = res3.results[c]["d_outp"]
    return out


# version 14 (final)
# speedup vs baseline: 1.0469x; 1.0469x over previous
import numpy as np
import ml_dtypes
import contextlib
import concourse.bass as bass
import concourse.mybir as mybir
from concourse.bass_utils import run_bass_kernel_spmd
from concourse.alu_op_type import AluOpType as Op

F32 = mybir.dt.float32
BF16 = mybir.dt.bfloat16
AF = mybir.ActivationFunctionType
BF = ml_dtypes.bfloat16
U8 = mybir.dt.uint8

B, L, D = 4, 8192, 1024
TPC = 4096
CH = 128
NCHUNK = TPC // CH
NTILE = TPC // 512
RC = 12582912.0

MODE = "fused"
L0_CUT = 99
ENG_ATTR = {"pe": "tensor", "act": "scalar", "dve": "vector", "pool": "gpsimd", "sp": "sync"}
ENGS = tuple(ENG_ATTR)


class Sched:
    def __init__(self, nc):
        self.nc = nc
        self.ops = []
        self.res = {}
        self.psum_keys = set()
        self.ns = ""
        self.barrier_deps = []

    def barrier(self):
        last = {}
        for i, o in enumerate(self.ops):
            if o["dma"] or o["cc"]:
                last[("d", o["eng"], o["wkeys"][0] if o["wkeys"] else i)] = i
            else:
                last[o["eng"]] = i
        self.barrier_deps = sorted(last.values())

    def op(self, eng, fn, r=(), w=(), dma=False, cc=False, relax=False):
        idx = len(self.ops)
        r = [self.ns + k for k in r]
        w = [self.ns + k for k in w]
        for k in list(r):
            if k in self.psum_keys:
                r.remove(k)
                if k not in w:
                    w.append(k)
        deps = set()
        for k in r:
            st = self.res.get(k)
            if st and st["w"] is not None:
                deps.add(st["w"])
        for k in w:
            st = self.res.get(k)
            if st:
                if st["w"] is not None:
                    deps.add(st["w"])
                deps.update(st["r"])
        for k in r:
            self.res.setdefault(k, {"w": None, "r": []})["r"].append(idx)
        for k in w:
            self.res[k] = {"w": idx, "r": []}
        deps.update(self.barrier_deps)
        deps.discard(idx)
        self.ops.append(dict(eng=eng, fn=fn, deps=sorted(deps), dma=dma, cc=cc,
                             wkeys=list(w), relax=relax))
        return idx

    def dma(self, eng, out, in_, r=(), w=()):
        self.op(eng, lambda e: e.dma_start(out=out, in_=in_), r=r, w=w, dma=True)

    def mm(self, out, lhsT, rhs, r, w, start=True, stop=True, tp=None, skip=False):
        self.op("pe", lambda e: e.matmul(out, lhsT, rhs, start=start, stop=stop,
                                         tile_position=tp, skip_group_check=skip),
                r=r, w=w, relax=not start)

    def trans(self, out, in_, ident, r, w):
        self.op("pe", lambda e: e.transpose(out, in_, ident), r=r, w=w)

    def cp(self, eng, out, in_, r, w):
        if eng == "act":
            self.op(eng, lambda e: e.copy(out, in_), r=r, w=w)
        else:
            self.op(eng, lambda e: e.tensor_copy(out, in_), r=r, w=w)

    def ts(self, eng, out, in0, s1, s2, op0, op1, r, w):
        self.op(eng, lambda e: e.tensor_scalar(out, in0, s1, s2, op0, op1), r=r, w=w)

    def tt(self, out, in0, in1, op, r, w):
        self.op("dve", lambda e: e.tensor_tensor(out, in0, in1, op), r=r, w=w)

    def stt(self, out, in0, sc, in1, op0, op1, r, w):
        self.op("dve", lambda e: e.scalar_tensor_tensor(out, in0, sc, in1, op0, op1), r=r, w=w)

    def act(self, out, in_, func, r, w, bias=None, scale=None, accum_out=None):
        kw = {}
        if bias is not None:
            kw["bias"] = bias
        if scale is not None:
            kw["scale"] = scale
        if accum_out is not None:
            kw["accum_out"] = accum_out
        self.op("act", lambda e: e.activation(out, in_, func, **kw), r=r, w=w)

    def red(self, out, in_, op, r, w, absv=None):
        self.op("dve", lambda e: e.tensor_reduce(out, in_, mybir.AxisListType.X, op,
                                                 apply_absolute_value=absv), r=r, w=w)

    def recip(self, out, in_, r, w):
        self.op("dve", lambda e: e.reciprocal(out, in_), r=r, w=w)

    def memset(self, eng, ap, val, w):
        self.op(eng, lambda e: e.memset(ap, val), r=(), w=w)

    def cpred(self, out, mask, data, r, w):
        self.op("dve", lambda e: e.copy_predicated(out, mask, data), r=r, w=w)

    def collective(self, ins, outs, r, w):
        self.op("pool", lambda e: e.collective_compute(
            "AllReduce", Op.add, replica_groups=[[0, 1], [2, 3], [4, 5], [6, 7]],
            ins=ins, outs=outs), r=r, w=w, cc=True)

    def finalize(self, es):
        nc = self.nc
        def dkey(i, o):
            wk = o["wkeys"][0] if o["wkeys"] else f"anon{i}"
            return ("dma", o["eng"], wk)
        sems = {}
        for e in ENGS:
            sems[(e, "c")] = es.enter_context(nc.semaphore(f"s_{e}_c"))
        dma_keys = []
        for i, o in enumerate(self.ops):
            if o["dma"]:
                k = dkey(i, o)
                if k not in sems:
                    nm = f"sd_{o['eng']}_{k[2]}".replace(":", "_")
                    sems[k] = es.enter_context(nc.semaphore(nm))
                    dma_keys.append(k)
        n_cc = sum(1 for o in self.ops if o["cc"])
        cc_sems = [es.enter_context(nc.semaphore(f"s_cc{i}")) for i in range(n_cc)]

        def is_async(d):
            return self.ops[d]["dma"] or self.ops[d]["cc"]

        for i, o in enumerate(self.ops):
            if not o.get("relax"):
                continue
            o["deps"] = [d for d in o["deps"]
                         if is_async(d) or self.ops[d]["eng"] != o["eng"]]
        needed = set()
        for o in self.ops:
            needed.update(o["deps"])
        tok, cnt, cc_i = {}, {}, 0
        for i, o in enumerate(self.ops):
            if o["cc"]:
                tok[i] = (("cc", cc_i), 1)
                cc_i += 1
            elif o["dma"]:
                key = dkey(i, o)
                cnt[key] = cnt.get(key, 0) + 16
                tok[i] = (key, cnt[key])
            elif i in needed:
                key = (o["eng"], "c")
                cnt[key] = cnt.get(key, 0) + 1
                tok[i] = (key, cnt[key])
        per_eng = {e: [] for e in ENGS}
        waited = {e: {} for e in ENGS}
        for i, o in enumerate(self.ops):
            waits = {}
            for d in o["deps"]:
                key, val = tok[d]
                if waits.get(key, 0) < val:
                    waits[key] = val
            todo = []
            wmap = waited[o["eng"]]
            for key, val in sorted(waits.items(), key=lambda kv: str(kv[0])):
                if wmap.get(key, 0) < val:
                    todo.append((key, val))
                    wmap[key] = val
            per_eng[o["eng"]].append((todo, o["fn"], tok.get(i)))

        fin = [(k, cnt[k]) for k in dma_keys]
        fin += [(("cc", i), 1) for i in range(n_cc)]
        wmap = waited["sp"]
        todo = [(k, v) for k, v in fin if wmap.get(k, 0) < v]
        per_eng["sp"].append((todo, lambda e: e.nop(), None))

        def sem_of(key):
            return cc_sems[key[1]] if key[0] == "cc" else sems[key]

        block = es.enter_context(nc.Block())

        def make_body(ops):
            def body(e):
                for waits, fn, inc in ops:
                    for key, val in waits:
                        e.wait_ge(sem_of(key), val)
                    inst = fn(e)
                    if inc is not None:
                        key, _ = inc
                        if key[0] == "cc":
                            inst.then_inc(sem_of(key))
                        else:
                            inst.then_inc(sem_of(key), 16 if key[0] == "dma" else 1)
            return body

        for e in ENGS:
            if per_eng[e]:
                getattr(block, ENG_ATTR[e])(make_body(per_eng[e]))


class TProxy:

    def __init__(self, h, name):
        self._h = h
        self.name = name

    def __getitem__(self, k):
        return self._h[k]

    def __getattr__(self, a):
        return getattr(self._h, a)


class Alloc:
    def __init__(self, nc, es, sched=None, prefix=""):
        self.nc, self.es, self.t = nc, es, {}
        self.sched = sched
        self.prefix = prefix

    def sb(self, name, shape, dt):
        h = self.es.enter_context(self.nc.sbuf_tensor(self.prefix + name, shape, dt))
        p = TProxy(h, name)
        self.t[name] = p
        return p

    def ps(self, name, shape, dt):
        h = self.es.enter_context(self.nc.psum_tensor(self.prefix + name, shape, dt))
        p = TProxy(h, name)
        self.t[name] = p
        if self.sched is not None:
            self.sched.psum_keys.add(self.sched.ns + name)
        return p


def host_constants(ins):
    c = {}
    c["ident"] = np.eye(128, dtype=BF)
    c["identf"] = np.eye(128, dtype=np.float32)
    for s, nm in ((16, "tri16"), (32, "tri32"), (64, "tri64"), (128, "tri128")):
        m = np.zeros((128, 128), np.float32)
        for t in range(128):
            m[(t // s) * s:t + 1, t] = 1.0
        c[nm] = m
    for s, nm in ((16, "rev16"), (32, "rev32"), (64, "rev64"), (128, "rev128")):
        m = np.zeros((128, 128), np.float32)
        for t in range(128):
            m[t + 1:(t // s) * s + s, t] = 1.0
        c[nm] = m
    c["amask"] = np.tile(np.triu(np.ones((128, 128), np.uint8)), (1, 8))
    c["onesrow"] = np.ones((1, 128), np.float32)
    c["cvals"] = np.tile(np.array([1e-5, 1.0, 1e-6, 0.0], np.float32)[None, :], (128, 1))
    c["onesb"] = np.ones((128, 1), dtype=BF)
    c["dtbb"] = np.tile(np.asarray(ins["L0_attn_dt_b"], np.float32)[None, :], (128, 1))
    c["bdb"] = np.tile(np.asarray(ins["L0_attn_base_decay"], np.float32).reshape(-1)[None, :], (128, 1))
    c["tfc"] = np.ascontiguousarray(
        np.asarray(ins["L0_attn_time_first"], np.float32).reshape(512).reshape(4, 128).T)
    c["wfTd"] = np.ascontiguousarray(
        np.vstack([ins["L1_attn_Wf"].T, ins["L1_attn_Wf"].T]).astype(np.float32))
    c["ebias"] = np.tile(np.asarray(ins["L1_ffn_e_bias"], np.float32)[None, :], (128, 1))
    ind = np.zeros((16, 1024), np.float32)
    for pair in range(8):
        for p in range(128):
            ind[2 * pair + p // 64, 128 * pair + p] = 1.0
    c["indn"] = ind.astype(BF)
    return c


def emit_l0(s, al, io):
    sb, ps = al.sb, al.ps

    ident = sb("ident", [128, 128], BF16)
    tri = {}
    for nm in ("tri16", "tri32", "tri64", "tri128", "rev16", "rev32", "rev64", "rev128"):
        tri[nm] = sb(nm, [128, 128], F32)
    amask = sb("amask", [128, 1024], U8)
    onesrow = sb("onesrow", [1, 128], F32)
    cvals = sb("cvals", [128, 4], F32)
    onesb = sb("onesb", [128, 1], BF16)
    dtbb = sb("dtbb", [128, 512], F32)
    bdb = sb("bdb", [128, 512], F32)
    tfc = sb("tfc", [128, 4], F32)
    sendm = sb("sendm", [128, 1], F32)
    wsel_a = sb("wsel_a", [128, 8, 128], BF16)
    wsel_b = sb("wsel_b", [128, 8, 32], BF16)
    dtw = sb("dtw", [32, 512], BF16)
    wk = sb("wk", [128, 8, 512], BF16)
    wv = sb("wv", [128, 8, 512], BF16)
    wr = sb("wr", [128, 8, 512], BF16)
    wg = sb("wg", [128, 8, 512], BF16)
    wo = sb("wo", [128, 4, 1024], BF16)
    qrhs = sb("qrhs", [128, 8, 1024], BF16)
    srhsb = sb("srhsb", [128, 1024], F32)
    sabs = sb("sabs", [128, 8], F32)
    smax = sb("smax", [128, 8], F32)
    srin = sb("srin", [128, 8], F32)
    srhs_c = sb("srhs_c", [128, 8], F32)

    stt_ = [sb(f"st{q}", [128, 64], F32) for q in range(4)]
    stb_ = [sb(f"stb{q}", [128, 64], BF16) for q in range(4)]
    stmp = sb("stmp", [128, 64], F32)
    scarry = sb("scarry", [128, 4], F32)
    pglob = sb("pglob", [128, 4], F32)
    stend = sb("stend", [128, 256], F32)

    xtm = sb("xtm", [128, 4, 1024], F32)
    sq32 = sb("sq32", [128, 1024], F32)
    q32 = [sb(f"q32_{i}", [128, 1024], F32) for i in range(2)]
    xnb = [sb(f"xnb{i}", [128, 1024], BF16) for i in range(2)]
    qtm = [sb(f"qtm{i}", [128, 1024], BF16) for i in range(2)]
    vec = {nm: sb(nm, [128, 4], F32) for nm in
           ("lsum", "lsq", "mean", "var", "rstd", "amax", "qrs", "slhs", "m2")}
    xnfm = sb("xnfm", [128, 8, 512], BF16)
    qfm = sb("qfm", [128, 8, 512], BF16)
    projfm_a = sb("projfm_a", [128, 512], BF16)
    projfm_b = sb("projfm_b", [32, 512], BF16)
    bsc = sb("bsc", [128, 4], F32)
    kfm = sb("kfm", [128, 4, 512], BF16)
    ktm = sb("ktm", [128, 4, 512], BF16)
    vtm = sb("vtm", [128, 4, 512], BF16)
    vhtm = sb("vhtm", [128, 4, 512], BF16)
    vfm = sb("vfm", [128, 4, 512], BF16)
    tfx = sb("tfx", [128, 4, 128], BF16)
    dtz = sb("dtz", [128, 512], F32)
    dt1 = sb("dt1", [128, 512], F32)
    rates = sb("rates", [128, 4, 512], F32)

    sf16 = sb("sf16", [128, 512], F32)
    sf128 = sb("sf128", [128, 512], F32)
    sc16 = sb("sc16", [128, 512], F32)
    c8t = sb("c8t", [128, 32], F32)
    ex1 = [sb(f"ex1_{i}", [128, 512], F32) for i in range(2)]
    tendt = sb("tendt", [128, 512], F32)
    khat = sb("khat", [128, 512], BF16)
    fac = {nm: sb("f_" + nm, [128, 512], BF16) for nm in
           ("qt", "aD", "bD", "a16", "b16", "a32", "b32", "a64", "b64")}
    qtg = sb("qtg", [128, 512], BF16)
    asb = sb("asb", [128, 1024], BF16)
    dcol = sb("dcol", [128, 4], F32)
    tfkv = sb("tfkv", [128, 4, 128], BF16)
    wkvtf = sb("wkvtf", [128, 4, 512], BF16)
    trr = sb("trr", [128, 512], F32)
    srgt = sb("srgt", [128, 4, 512], BF16)
    x1t = [sb("x1t0", [128, 1024], F32)]

    psa = ps("psa", [128, 512], F32)
    psb = ps("psb", [128, 512], F32)
    psc = ps("psc", [128, 512], F32)
    psA = ps("psA", [128, 1024], F32)
    psw = ps("psw", [128, 512], F32)
    pstA = ps("pstA", [128, 1024], BF16)
    pstB = ps("pstB", [128, 1024], BF16)

    for nm, buf in [("ident", ident), ("amask", amask), ("onesrow", onesrow),
                    ("cvals", cvals),
                    ("onesb", onesb), ("dtbb", dtbb), ("bdb", bdb), ("tfc", tfc),
                    ("send_mask", sendm)] + [(k, tri[k]) for k in tri]:
        s.dma("sp", buf[:], io[nm], w=[buf.name])
    wsel_r = io["Wsel"].rearrange("(c p) m -> p c m", p=128)
    s.dma("pool", wsel_a[:], wsel_r[:, :, 0:128], w=["wsel_a"])
    s.dma("pool", wsel_b[:], wsel_r[:, :, 128:160], w=["wsel_b"])
    s.dma("pool", dtw[:], io["dtW"], w=["dtw"])
    for nm, buf in (("Wk", wk), ("Wv", wv), ("Wr", wr), ("Wg", wg)):
        s.dma("pool", buf[:], io[nm].rearrange("(c p) m -> p c m", p=128), w=[buf.name])
    s.dma("pool", wo[:], io["Wo"].rearrange("(c p) m -> p c m", p=128), w=["wo"])

    if L0_CUT < 1:
        return
    rT = io["rhsT"].rearrange("(c p) m -> p c m", p=128)
    for half in range(2):
        xkey = "xtB"
        s.dma("sp", xtm[:], rT[:, 4 * half:4 * half + 4, :], w=["xtA", "xtB"])
        for j in range(4):
            jc = 4 * half + j
            s.red(sabs[:, jc:jc + 1], xtm[:, j, :], Op.max, r=[xkey],
                  w=["sabs"], absv=True)
        s.ts("dve", smax[:, 4 * half:4 * half + 4], sabs[:, 4 * half:4 * half + 4],
             1e-12, None, Op.max, Op.bypass, r=["sabs"], w=["smax"])
        s.recip(srin[:, 4 * half:4 * half + 4], smax[:, 4 * half:4 * half + 4],
                r=["smax"], w=["srin"])
        s.ts("dve", srhs_c[:, 4 * half:4 * half + 4], smax[:, 4 * half:4 * half + 4],
             1.0 / 127.0, None, Op.mult, Op.bypass, r=["smax"], w=["srhs_c"])
        for j in range(4):
            jc = 4 * half + j
            s.ts("dve", sq32[:], xtm[:, j, :], srin[:, jc:jc + 1], 127.0,
                 Op.mult, Op.mult, r=[xkey, "srin"], w=["sq32"])
            s.ts("dve", sq32[:], sq32[:], RC, RC, Op.add, Op.subtract, r=["sq32"], w=["sq32"])
            s.ts("dve", xnb[0][:], sq32[:], 127.0, -127.0, Op.min, Op.max,
                 r=["sq32"], w=["xnb0"])
            for dc in range(8):
                pt = pstA if dc % 2 == 0 else pstB
                s.trans(pt[:, 0:128], xnb[0][:, 128 * dc:128 * (dc + 1)],
                        ident[:], r=["xnb0", "ident"], w=[pt.name])
                s.cp("dve", qrhs[:, dc, 128 * jc:128 * (jc + 1)], pt[:, 0:128],
                     r=[pt.name], w=["qrhs"])
    for jc in range(8):
        s.dma("act", io["scr8"][jc], srhs_c[:, jc:jc + 1], r=["srhs_c"], w=["scr8d"])
    s.dma("sp", sq32[0:1, :], io["scr8"].rearrange("a b -> (a b)").unsqueeze(0),
          r=["scr8d"], w=["sq32"])
    for half in range(2):
        s.mm(psa[:, 0:512], onesrow[0:1, :], sq32[0:1, 512 * half:512 * (half + 1)],
             r=["sq32", "onesrow"], w=["psa"])
        s.cp("dve", srhsb[:, 512 * half:512 * (half + 1)], psa[:, 0:512],
             r=["psa"], w=["srhsb"])

    if L0_CUT < 2:
        return
    s.memset("dve", asb[:], 0.0, w=["asb"])
    s.memset("dve", psA[:], 0.0, w=["psA"])
    for q in range(4):
        s.memset("dve", stt_[q][:], 0.0, w=[f"st{q}"])
        s.memset("dve", stb_[q][:], 0.0, w=[f"stb{q}"])
    s.memset("dve", scarry[:], 0.0, w=["scarry"])

    x_dram, x1_dram = io["x"], io["x1"]
    qg_dram, srg_dram = io["qglob"], io["srg"]

    def xload(g, half):
        rows = slice(g * 512 + half * 256, g * 512 + half * 256 + 256)
        s.dma("sp", xtm[:, 2 * half:2 * half + 2, :],
              x_dram[rows, :].rearrange("(c p) d -> p c d", p=128),
              w=["xtA" if half == 0 else "xtB"])
    xload(0, 0)
    xload(0, 1)
    for g in range(NTILE):
        rows = slice(g * 512, (g + 1) * 512)
        for tc in range(4):
            xkey = "xtA" if tc < 2 else "xtB"
            xs = xtm[:, tc, :]
            s.red(vec["lsum"][:, tc:tc + 1], xs, Op.add, r=[xkey], w=["lsum"])
            s.tt(sq32[:], xs, xs, Op.mult, r=[xkey], w=["sq32"])
            s.red(vec["lsq"][:, tc:tc + 1], sq32[:], Op.add, r=["sq32"], w=["lsq"])
        s.ts("dve", vec["mean"][:], vec["lsum"][:], 1.0 / D, None, Op.mult, Op.bypass,
             r=["lsum"], w=["mean"])
        s.tt(vec["m2"][:], vec["mean"][:], vec["mean"][:], Op.mult, r=["mean"], w=["m2"])
        s.ts("dve", vec["var"][:], vec["lsq"][:], 1.0 / D, None, Op.mult, Op.bypass,
             r=["lsq"], w=["var"])
        s.tt(vec["var"][:], vec["var"][:], vec["m2"][:], Op.subtract, r=["var", "m2"], w=["var"])
        s.act(vec["rstd"][:], vec["var"][:], AF.Ln, r=["var", "cvals"], w=["rstd"], bias=cvals[:, 0:1])
        s.act(vec["rstd"][:], vec["rstd"][:], AF.Exp, r=["rstd"], w=["rstd"], scale=-0.5)
        for tc in range(4):
            sl = tc % 2
            kxn, kq, kqt = f"xnb{sl}", f"q32_{sl}", f"qtm{sl}"
            s.ts("dve", sq32[:], xtm[:, tc, :], vec["mean"][:, tc:tc + 1],
                 vec["rstd"][:, tc:tc + 1], Op.subtract, Op.mult,
                 r=["xtA" if tc < 2 else "xtB", "mean", "rstd"], w=["sq32"])
            s.cp("dve", xnb[sl][:], sq32[:], r=["sq32"], w=[kxn])
            s.red(vec["amax"][:, tc:tc + 1], sq32[:], Op.max, r=["sq32"], w=["amax"], absv=True)
            s.ts("dve", vec["amax"][:, tc:tc + 1], vec["amax"][:, tc:tc + 1], 1e-12, None,
                 Op.max, Op.bypass, r=["amax"], w=["amax"])
            s.recip(vec["qrs"][:, tc:tc + 1], vec["amax"][:, tc:tc + 1], r=["amax"], w=["qrs"])
            s.ts("dve", q32[sl][:], sq32[:], vec["qrs"][:, tc:tc + 1], None, Op.mult,
                 Op.bypass, r=["sq32", "qrs"], w=[kq])
            s.ts("pool", q32[sl][:], q32[sl][:], 127.0, RC, Op.mult, Op.add, r=[kq], w=[kq])
            s.ts("pool", q32[sl][:], q32[sl][:], -RC, 1.0, Op.add, Op.mult, r=[kq], w=[kq])
            s.ts("pool", qtm[sl][:], q32[sl][:], 127.0, -127.0, Op.min, Op.max,
                 r=[kq], w=[kqt])
            for dc in range(8):
                pt = pstA if dc % 2 == 0 else pstB
                s.trans(pt[:, 0:128], xnb[sl][:, 128 * dc:128 * (dc + 1)],
                        ident[:], r=[kxn, "ident"], w=[pt.name])
                s.cp("dve", xnfm[:, dc, 128 * tc:128 * (tc + 1)], pt[:, 0:128],
                     r=[pt.name], w=["xnfm"])
                pt2 = pstB if dc % 2 == 0 else pstA
                s.trans(pt2[:, 128:256], qtm[sl][:, 128 * dc:128 * (dc + 1)],
                        ident[:], r=[kqt, "ident"], w=[pt2.name])
                s.cp("dve", qfm[:, dc, 128 * tc:128 * (tc + 1)], pt2[:, 128:256],
                     r=[pt2.name], w=["qfm"])
        s.ts("dve", vec["slhs"][:], vec["amax"][:], 1.0 / 127.0, None, Op.mult, Op.bypass,
             r=["amax"], w=["slhs"])
        if L0_CUT < 3:
            continue
        for kc in range(8):
            s.mm(psa[:, 0:512], wsel_a[:, kc, :], xnfm[:, kc, :], r=["wsel_a", "xnfm"],
                 w=["psa"], start=(kc == 0), stop=(kc == 7))
            s.mm(psb[0:32, 0:512], wsel_b[:, kc, :], xnfm[:, kc, :], r=["wsel_b", "xnfm"],
                 w=["psb"], start=(kc == 0), stop=(kc == 7))
        s.cp("dve", projfm_a[:], psa[:, 0:512], r=["psa"], w=["projfm_a"])
        s.cp("dve", projfm_b[:], psb[0:32, 0:512], r=["psb"], w=["projfm_b"])
        if L0_CUT < 3.2:
            continue
        for tc in range(4):
            cols = slice(128 * tc, 128 * (tc + 1))
            s.mm(psc[0:128, 0:1], projfm_a[32:64, cols], onesb[32:64, :],
                 r=["projfm_a", "onesb"], w=["psc"], start=True, stop=False, tp=(32, 0))
            s.mm(psc[0:128, 0:1], projfm_a[64:96, cols], onesb[64:96, :],
                 r=["projfm_a", "onesb"], w=["psc"], start=False, stop=True, tp=(64, 0))
            s.act(bsc[:, tc:tc + 1], psc[:, 0:1], AF.Copy, r=["psc"], w=["bsc"],
                  scale=1.0 / 64.0)
        if L0_CUT < 3.4:
            continue
        for tc in range(4):
            cols = slice(128 * tc, 128 * (tc + 1))
            s.mm(psa[:, 0:512], projfm_a[0:32, cols], dtw[:], r=["projfm_a", "dtw"], w=["psa"])
            s.tt(dtz[:], psa[:, 0:512], dtbb[:], Op.add, r=["psa", "dtbb"], w=["dtz"])
            s.act(dt1[:], dtz[:], AF.Abs, r=["dtz"], w=["dt1"])
            s.act(dt1[:], dt1[:], AF.Exp, r=["dt1"], w=["dt1"], scale=-1.0)
            s.act(dt1[:], dt1[:], AF.Ln, r=["dt1", "cvals"], w=["dt1"], bias=cvals[:, 1:2])
            s.stt(dt1[:], dtz[:], 0.0, dt1[:], Op.max, Op.add, r=["dtz", "dt1"], w=["dt1"])
            s.tt(rates[:, tc, :], dt1[:], bdb[:], Op.mult, r=["dt1", "bdb"], w=["rates"])
        if L0_CUT < 3.45:
            continue
        for mc in range(4):
            pd = [psa, psb, psc, psa][mc]
            for kc in range(8):
                s.mm(pd[:, 0:512], wk[:, kc, 128 * mc:128 * (mc + 1)], xnfm[:, kc, :],
                     r=["wk", "xnfm"], w=[pd.name], start=(kc == 0), stop=(kc == 7))
            s.cp("dve", kfm[:, mc, :], pd[:, 0:512], r=[pd.name], w=["kfm"])
        if L0_CUT < 3.56:
            continue
        for tc in range(4):
            pd = [psa, psb, psc, psa][tc]
            for kc in range(8):
                s.mm(pd[:, 0:512], xnfm[:, kc, 128 * tc:128 * (tc + 1)], wv[:, kc, :],
                     r=["wv", "xnfm"], w=[pd.name], start=(kc == 0), stop=(kc == 7))
            s.cp("dve", vtm[:, tc, :], pd[:, 0:512], r=[pd.name], w=["vtm"])
            if L0_CUT < 3.62:
                continue
            s.ts("dve", vhtm[:, tc, :], pd[:, 0:512], bsc[:, tc:tc + 1], None,
                 Op.mult, Op.bypass, r=[pd.name, "bsc"], w=["vhtm"])
        for mc in range(4):
            for tc in range(4):
                pt = pstA if (tc % 2 == 0) else pstB

                s.trans(pt[:, 0:128], kfm[:, mc, 128 * tc:128 * (tc + 1)], ident[:],

                        r=["kfm", "ident"], w=[pt.name])

                s.cp("dve", ktm[:, tc, 128 * mc:128 * (mc + 1)], pt[:, 0:128],

                     r=[pt.name], w=["ktm"])
                pt = pstB if (tc % 2 == 0) else pstA

                s.trans(pt[:, 128:256], vtm[:, tc, 128 * mc:128 * (mc + 1)], ident[:],

                        r=["vtm", "ident"], w=[pt.name])

                s.cp("dve", vfm[:, mc, 128 * tc:128 * (tc + 1)], pt[:, 128:256],

                     r=[pt.name], w=["vfm"])

        if L0_CUT < 4:
            continue
        for cc in range(4):
            ci = g * 4 + cc
            rsl = rates[:, cc, :]
            ksl = kfm[:, :, 128 * cc:128 * (cc + 1)]

            def svar(nm, trik, pd, pkey, save=None):
                for q in range(4):
                    s.mm(pd[:, 128 * q:128 * (q + 1)], rsl[:, 128 * q:128 * (q + 1)],
                         tri[trik][:], r=["rates", trik], w=[pkey],
                         start=True, stop=True, skip=True)
                if save is not None:
                    s.cp("dve", save[0][:], pd[:, 0:512], r=[pkey], w=[save[1]])

            def facmake(nm, pd, pkey, scl, ei):
                if L0_CUT < 4.07:
                    return
                s.act(ex1[ei][:], pd[:, 0:512], AF.Exp, r=[pkey], w=[f"ex1_{ei}"], scale=scl)
                if L0_CUT < 4.09:
                    return
                s.op("dve", lambda e, nm=nm, ei=ei, ksl=ksl: e.tensor_tensor(
                    fac[nm][:].rearrange("p (a t) -> p a t", a=4), ksl,
                    ex1[ei][:].rearrange("p (a t) -> p a t", a=4), Op.mult),
                    r=["kfm", f"ex1_{ei}"], w=["f_" + nm])

            svar("S16", "tri16", psa, "psa", save=(sf16, "sf16"))
            facmake("a16", psa, "psa", -1.0, 0)
            svar("S32", "tri32", psb, "psb")
            facmake("a32", psb, "psb", -1.0, 1)
            svar("S64", "tri64", psa, "psa")
            facmake("a64", psa, "psa", -1.0, 0)
            svar("S128", "tri128", psb, "psb", save=(sf128, "sf128"))
            facmake("qt", psb, "psb", -1.0, 1)
            svar("R16", "rev16", psa, "psa")
            facmake("b16", psa, "psa", -1.0, 0)
            svar("R32", "rev32", psb, "psb")
            facmake("b32", psb, "psb", -1.0, 1)
            svar("R64", "rev64", psa, "psa")
            facmake("b64", psa, "psa", -1.0, 0)
            if L0_CUT < 4.2:
                continue
            s.ts("dve", c8t[:], sf16[:, 15::16], 0.5, None, Op.mult, Op.bypass,
                 r=["sf16"], w=["c8t"])
            s.op("dve", lambda e: e.tensor_tensor(
                sc16[:].rearrange("p (a c) -> p a c", a=32),
                sf16[:].rearrange("p (a c) -> p a c", a=32),
                c8t[:].unsqueeze(2).broadcast_to([128, 32, 16]),
                Op.subtract), r=["sf16", "c8t"], w=["sc16"])
            s.act(ex1[0][:], sc16[:], AF.Exp, r=["sc16"], w=["ex1_0"], scale=-1.0)
            s.op("dve", lambda e, ksl=ksl: e.tensor_tensor(
                fac["aD"][:].rearrange("p (a t) -> p a t", a=4), ksl,
                ex1[0][:].rearrange("p (a t) -> p a t", a=4), Op.mult),
                r=["kfm", "ex1_0"], w=["f_aD"])
            s.act(ex1[1][:], sc16[:], AF.Exp, r=["sc16"], w=["ex1_1"], scale=1.0)
            s.op("dve", lambda e, ksl=ksl: e.tensor_tensor(
                fac["bD"][:].rearrange("p (a t) -> p a t", a=4), ksl,
                ex1[1][:].rearrange("p (a t) -> p a t", a=4), Op.mult),
                r=["kfm", "ex1_1"], w=["f_bD"])
            if L0_CUT < 4.3:
                continue
            s.mm(psc[:, 0:512], tri["rev128"][:], rsl, r=["rates", "rev128"], w=["psc"])
            s.act(tendt[:], psc[:, 0:512], AF.Exp, r=["psc"], w=["tendt"], scale=-1.0)
            s.tt(khat[:], ktm[:, cc, :], tendt[:], Op.mult, r=["ktm", "tendt"], w=["khat"])
            if L0_CUT < 4.4:
                continue
            for h in range(8):
                q, po = h // 2, 64 * (h % 2)
                base = 128 * h

                def fs(f, lo, ln):
                    return fac[f][po:po + 64, 128 * q + lo:128 * q + lo + ln]
                for bq in range(4):
                    s.mm(psA[32 * bq:32 * bq + 32, base + 32 * bq:base + 32 * bq + 32],
                         fs("bD", 32 * bq, 32), fs("aD", 32 * bq, 32),
                         r=["f_bD", "f_aD"], w=["psA"], skip=True, tp=(po, 32 * bq))
                for bq in range(4):
                    s.mm(psA[32 * bq:32 * bq + 16, base + 32 * bq + 16:base + 32 * bq + 32],
                         fs("b16", 32 * bq, 16), fs("a16", 32 * bq + 16, 16),
                         r=["f_b16", "f_a16"], w=["psA"], skip=True, tp=(po, 32 * bq))
                for bq in range(2):
                    s.mm(psA[64 * bq:64 * bq + 32, base + 64 * bq + 32:base + 64 * bq + 64],
                         fs("b32", 64 * bq, 32), fs("a32", 64 * bq + 32, 32),
                         r=["f_b32", "f_a32"], w=["psA"], skip=True, tp=(po, 64 * bq))
                s.mm(psA[0:64, base + 64:base + 128], fs("b64", 0, 64), fs("a64", 64, 64),
                     r=["f_b64", "f_a64"], w=["psA"], skip=True, tp=(po, 0))
            if L0_CUT < 4.45:
                continue
            s.cpred(asb[:], amask[:], psA[:], r=["psA", "amask"], w=["asb"])
            if L0_CUT < 4.5:
                continue
            s.act(pglob[:], scarry[:], AF.Exp, r=["scarry"], w=["pglob"], scale=-1.0)
            s.op("dve", lambda e: e.tensor_tensor(
                qtg[:].rearrange("p (a t) -> p a t", a=4),
                fac["qt"][:].rearrange("p (a t) -> p a t", a=4),
                pglob[:].unsqueeze(2).broadcast_to([128, 4, 128]),
                Op.mult), r=["f_qt", "pglob"], w=["qtg"])
            s.dma("act", qg_dram[ci], qtg[:], r=["qtg"], w=["qgd"])
            s.tt(scarry[:], scarry[:], sf128[:, 127::128], Op.add,
                 r=["scarry", "sf128"], w=["scarry"])
            s.act(dcol[:], sf128[:, 127::128], AF.Exp, r=["sf128"], w=["dcol"], scale=-1.0)
            if L0_CUT < 4.6:
                continue
            for h in range(8):
                q, po = h // 2, 64 * (h % 2)
                dst = psw[po:po + 64, 128 * q:128 * (q + 1)]
                s.mm(dst, stb_[q][po:po + 64, :], fac["qt"][po:po + 64, 128 * q:128 * (q + 1)],
                     r=[f"stb{q}", "f_qt"], w=["psw"], start=True, stop=False, skip=True,
                     tp=(po, po))
                s.mm(dst, vhtm[:, cc, 64 * h:64 * h + 64], asb[:, 128 * h:128 * (h + 1)],
                     r=["vhtm", "asb"], w=["psw"], start=False, stop=True, skip=True,
                     tp=(0, po))
            s.op("dve", lambda e, cc=cc: e.tensor_tensor(
                tfx[:], kfm[:, :, 128 * cc:128 * (cc + 1)],
                tfc[:].unsqueeze(2).broadcast_to([128, 4, 128]), Op.mult),
                r=["kfm", "tfc"], w=["tfx"])
            s.op("dve", lambda e, cc=cc: e.tensor_tensor(
                tfkv[:], tfx[:], vfm[:, :, 128 * cc:128 * (cc + 1)], Op.mult),
                r=["tfx", "vfm"], w=["tfkv"])
            s.op("dve", lambda e, cc=cc: e.tensor_tensor(
                wkvtf[:, :, 128 * cc:128 * (cc + 1)],
                psw[:].rearrange("p (a t) -> p a t", a=4), tfkv[:], Op.add),
                r=["psw", "tfkv"], w=["wkvtf"])
            if L0_CUT < 4.8:
                continue
            for h in range(8):
                q, po = h // 2, 64 * (h % 2)
                s.mm(psc[po:po + 64, 64 * q:64 * q + 64], khat[:, 64 * h:64 * h + 64],
                     vhtm[:, cc, 64 * h:64 * h + 64], r=["khat", "vhtm"], w=["psc"],
                     skip=True, tp=(0, po))
            for q in range(4):
                s.ts("dve", stmp[:], stt_[q][:], dcol[:, q:q + 1], None, Op.mult, Op.bypass,
                     r=[f"st{q}", "dcol"], w=["stmp"])
                s.tt(stt_[q][:], stmp[:], psc[:, 64 * q:64 * q + 64], Op.add,
                     r=["stmp", "psc"], w=[f"st{q}"])
                s.cp("dve", stb_[q][:], stt_[q][:], r=[f"st{q}"], w=[f"stb{q}"])

        if L0_CUT < 5:
            continue
        for mc in range(4):
            pd = [psa, psb][mc % 2]
            for kc in range(8):
                s.mm(pd[:, 0:512], wr[:, kc, 128 * mc:128 * (mc + 1)], xnfm[:, kc, :],
                     r=["wr", "xnfm"], w=[pd.name], start=(kc == 0), stop=(kc == 7))
            s.act(trr[:], pd[:, 0:512], AF.Tanh, r=[pd.name], w=["trr"], scale=0.5)
            pd2 = psc
            for kc in range(8):
                s.mm(pd2[:, 0:512], wg[:, kc, 128 * mc:128 * (mc + 1)], xnfm[:, kc, :],
                     r=["wg", "xnfm"], w=[pd2.name], start=(kc == 0), stop=(kc == 7))
            s.act(dtz[:], pd2[:, 0:512], AF.Tanh, r=[pd2.name], w=["dtz"], scale=0.5)
            s.ts("dve", dt1[:], trr[:], 1.0, None, Op.add, Op.bypass, r=["trr"], w=["dt1"])
            s.stt(srgt[:, mc, :], dtz[:], 1.0, dt1[:], Op.add, Op.mult,
                  r=["dtz", "dt1"], w=["srgt"])
            s.stt(wkvtf[:, mc, :], srgt[:, mc, :], 0.25, wkvtf[:, mc, :], Op.mult, Op.mult,
                  r=["srgt", "wkvtf"], w=["wkvtf"])
        s.dma("act", srg_dram[g], srgt[:], r=["srgt"], w=["srgd"])
        for tc in range(4):
            sl = 0
            cols = slice(128 * tc, 128 * (tc + 1))
            for half in range(2):
                pa_, pf_ = (psa, psb) if half == 0 else (psc, psa)
                hc = slice(512 * half, 512 * (half + 1))
                for kc in range(4):
                    s.mm(pa_[:, 0:512], wkvtf[:, kc, cols], wo[:, kc, 512 * half:512 * (half + 1)],
                         r=["wkvtf", "wo"], w=[pa_.name], start=(kc == 0), stop=(kc == 3))
                for kc in range(8):
                    s.mm(pf_[:, 0:512], qfm[:, kc, cols], qrhs[:, kc, 512 * half:512 * (half + 1)],
                         r=["qfm", "qrhs"], w=[pf_.name], start=(kc == 0), stop=(kc == 7))
                s.tt(x1t[sl][:, hc], xtm[:, tc, hc], pa_[:, 0:512], Op.add,
                     r=["xtA" if tc < 2 else "xtB", pa_.name], w=[f"x1t{sl}"])
                s.stt(trr[:], pf_[:, 0:512], vec["slhs"][:, tc:tc + 1], srhsb[:, hc],
                      Op.mult, Op.mult, r=[pf_.name, "slhs", "srhsb"], w=["trr"])
                s.tt(x1t[sl][:, hc], x1t[sl][:, hc], trr[:], Op.add,
                     r=[f"x1t{sl}", "trr"], w=[f"x1t{sl}"])
            rr = slice(g * 512 + tc * 128, g * 512 + tc * 128 + 128)
            s.dma("act", x1_dram[rr, :], x1t[sl][:], r=[f"x1t{sl}"], w=["x1d"])
            if g + 1 < NTILE and tc in (1, 3):
                xload(g + 1, 0 if tc == 1 else 1)

    if L0_CUT < 6:
        return
    for q in range(4):
        s.ts("dve", stend[:, 64 * q:64 * (q + 1)], stt_[q][:], sendm[:, 0:1], None,
             Op.mult, Op.bypass, r=[f"st{q}", "sendm"], w=["stend"])
    s.dma("pool", io["cc_in"], stend[:], r=["stend"], w=["ccin"])
    s.collective([io["cc_in"].opt()], [io["cc_out"].opt()], r=["ccin"], w=["ccout"])
    s.dma("pool", stend[:], io["cc_out"], r=["ccout"], w=["stend"])
    s.dma("pool", io["st_mid"], stend[:], r=["stend"], w=["stmidd"])


def emit_l0c(s, al, io):
    sb, ps = al.sb, al.ps
    wo = sb("wo_c", [128, 4, 1024], BF16)
    stm = sb("stm", [128, 256], F32)
    stmb = sb("stmb", [128, 256], BF16)
    usem = sb("usem", [128, 1], F32)
    qtg_ = [sb(f"qtg_c{i}", [128, 4, 512], BF16) for i in range(2)]
    srgt_ = [sb(f"srgt_c{i}", [128, 4, 512], BF16) for i in range(2)]
    dys = sb("dys", [128, 4, 512], BF16)
    x1t_ = [sb(f"x1t_c{i}", [128, 4, 1024], F32) for i in range(2)]
    psw = ps("psw_c", [128, 512], F32)
    psa = ps("psa_c", [128, 512], F32)
    psb = ps("psb_c", [128, 512], F32)

    s.dma("pool", wo[:], io["Wo"].rearrange("(c p) m -> p c m", p=128), w=["wo_c"])
    s.dma("sp", stm[:], io["st_mid_in"], w=["stm_raw"])
    s.dma("sp", usem[:], io["use_mask"], w=["usem"])
    s.ts("dve", stmb[:], stm[:], usem[:, 0:1], None, Op.mult, Op.bypass,
         r=["stm_raw", "usem"], w=["stmb"])

    def l0c_load(g):
        p = g % 2
        rows = slice(g * 512, (g + 1) * 512)
        s.dma("sp", x1t_[p][:], io["x1_in"][rows, :].rearrange("(c p) d -> p c d", p=128),
              w=[f"x1t_c{p}"])
        s.dma("sp", srgt_[p][:], io["srg_in"][g], w=[f"srgt_c{p}"])
        for cc in range(4):
            s.dma("sp", qtg_[p][:, cc, :], io["qglob_in"][g * 4 + cc], w=[f"qtg_c{p}"])
    l0c_load(0)
    for g in range(NTILE):
        rows = slice(g * 512, (g + 1) * 512)
        p = g % 2
        x1t, srgt, qtg = x1t_[p], srgt_[p], qtg_[p]
        xk, sk_, qk = f"x1t_c{p}", f"srgt_c{p}", f"qtg_c{p}"
        if g + 1 < NTILE:
            l0c_load(g + 1)
        for cc in range(4):
            for h in range(8):
                q, po = h // 2, 64 * (h % 2)
                s.mm(psw[po:po + 64, 128 * q:128 * (q + 1)],
                     stmb[po:po + 64, 64 * q:64 * q + 64],
                     qtg[po:po + 64, cc, 128 * q:128 * (q + 1)],
                     r=["stmb", qk], w=["psw_c"], skip=True, tp=(po, po))
            s.op("dve", lambda e, cc=cc: e.scalar_tensor_tensor(
                dys[:, :, 128 * cc:128 * (cc + 1)],
                srgt[:, :, 128 * cc:128 * (cc + 1)], 0.25,
                psw[:].rearrange("p (a t) -> p a t", a=4),
                Op.mult, Op.mult), r=[sk_, "psw_c"], w=["dys"])
        for tc in range(4):
            cols = slice(128 * tc, 128 * (tc + 1))
            for half, pd in ((0, psa), (1, psb)):
                for kc in range(4):
                    s.mm(pd[:, 0:512], dys[:, kc, cols], wo[:, kc, 512 * half:512 * (half + 1)],
                         r=["dys", "wo_c"], w=[pd.name], start=(kc == 0), stop=(kc == 3))
                s.tt(x1t[:, tc, 512 * half:512 * (half + 1)],
                     x1t[:, tc, 512 * half:512 * (half + 1)], pd[:, 0:512], Op.add,
                     r=[xk, pd.name], w=[xk])
        s.dma("act", io["x1c"][rows, :].rearrange("(c p) d -> p c d", p=128), x1t[:],
              r=[xk], w=["x1cd"])



def emit_ln(s, al, pref, xc_ap, xckey, xnfm_ap, xnfmkey, n_tc, psA_, psAkey, ident):
    t = al.t
    sq = t[pref + "sq"]
    xnb = t[pref + "xnb"]
    v_ = {nm: t[pref + nm] for nm in ("lsum", "lsq", "mean", "var", "rstd", "m2")}
    cvals = t["cvals"]
    for tc in range(n_tc):
        xs = xc_ap[:, tc, :]
        s.red(v_["lsum"][:, tc:tc + 1], xs, Op.add, r=[xckey], w=[pref + "lsum"])
        s.tt(sq[:], xs, xs, Op.mult, r=[xckey], w=[pref + "sq"])
        s.red(v_["lsq"][:, tc:tc + 1], sq[:], Op.add, r=[pref + "sq"], w=[pref + "lsq"])
    nt = slice(0, n_tc)
    s.ts("dve", v_["mean"][:, nt], v_["lsum"][:, nt], 1.0 / D, None, Op.mult, Op.bypass,
         r=[pref + "lsum"], w=[pref + "mean"])
    s.tt(v_["m2"][:, nt], v_["mean"][:, nt], v_["mean"][:, nt], Op.mult,
         r=[pref + "mean"], w=[pref + "m2"])
    s.ts("dve", v_["var"][:, nt], v_["lsq"][:, nt], 1.0 / D, None, Op.mult, Op.bypass,
         r=[pref + "lsq"], w=[pref + "var"])
    s.tt(v_["var"][:, nt], v_["var"][:, nt], v_["m2"][:, nt], Op.subtract,
         r=[pref + "var", pref + "m2"], w=[pref + "var"])
    s.act(v_["rstd"][:, nt], v_["var"][:, nt], AF.Ln, r=[pref + "var", "cvals"],
          w=[pref + "rstd"], bias=cvals[:, 0:1])
    s.act(v_["rstd"][:, nt], v_["rstd"][:, nt], AF.Exp, r=[pref + "rstd"],
          w=[pref + "rstd"], scale=-0.5)
    for tc in range(n_tc):
        s.ts("dve", sq[:], xc_ap[:, tc, :], v_["mean"][:, tc:tc + 1],
             v_["rstd"][:, tc:tc + 1], Op.subtract, Op.mult,
             r=[xckey, pref + "mean", pref + "rstd"], w=[pref + "sq"])
        s.cp("dve", xnb[:], sq[:], r=[pref + "sq"], w=[pref + "xnb"])
        for dc in range(8):
            s.trans(psA_[:, 0:128], xnb[:, 128 * dc:128 * (dc + 1)], ident[:],
                    r=[pref + "xnb", "ident"], w=[psAkey])
            s.cp("dve", xnfm_ap[:, dc, 128 * tc:128 * (tc + 1)], psA_[:, 0:128],
                 r=[psAkey], w=[xnfmkey])


def _ln_scratch(al, pref):
    al.sb(pref + "sq", [128, 1024], F32)
    al.sb(pref + "xnb", [128, 1024], BF16)
    for nm in ("lsum", "lsq", "mean", "var", "rstd", "m2"):
        al.sb(pref + nm, [128, 8], F32)


def emit_l1kv(s, al, io):
    sb, ps = al.sb, al.ps
    ident = sb("ident", [128, 128], BF16)
    cvals = sb("cvals", [128, 4], F32)
    onesb = sb("onesb", [128, 1], BF16)
    wkT = sb("wkT", [128, 8, 1024], BF16)
    wvT = sb("wvT", [128, 8, 1024], BF16)
    wfd = sb("wfd", [128, 256], BF16)
    xc_ = [sb(f"xc{i}", [128, 4, 1024], F32) for i in range(2)]
    xnfm = sb("xnfm", [128, 8, 512], BF16)
    kfm1 = sb("kfm1", [128, 8, 512], BF16)
    vtm1 = sb("vtm1", [128, 4, 1024], BF16)
    exs_ = [sb(f"exs{i}", [128, 512], F32) for i in range(2)]
    kft_ = [sb(f"kft{i}", [128, 512], BF16) for i in range(2)]
    kva = sb("kva", [128, 2, 1024], F32)
    ksa = sb("ksa", [128, 32], F32)
    _ln_scratch(al, "ln_")
    psa = ps("psa", [128, 512], F32)
    psb = ps("psb", [128, 512], F32)
    kvp0 = ps("kvp0", [128, 1024], F32)
    kvp1 = ps("kvp1", [128, 1024], F32)
    ksp = ps("ksp", [128, 512], F32)
    pstA = ps("pstA", [128, 1024], BF16)

    s.dma("sp", ident[:], io["ident"], w=["ident"])
    s.dma("sp", cvals[:], io["cvals"], w=["cvals"])
    s.dma("sp", onesb[:], io["onesb"], w=["onesb"])
    s.dma("pool", wkT[:], io["l1wkT"].rearrange("(c p) m -> p c m", p=128), w=["wkT"])
    s.dma("pool", wvT[:], io["l1wvT"].rearrange("(c p) m -> p c m", p=128), w=["wvT"])
    s.dma("pool", wfd[:], io["wfTd"], w=["wfd"])

    s.memset("dve", kva[:], 0.0, w=["kva"])
    s.memset("dve", ksa[:], 0.0, w=["ksa"])
    s.dma("sp", xc_[0][:], io["x1c_in"][0:512, :].rearrange("(c p) d -> p c d", p=128),
          w=["xc0"])
    for g in range(NTILE):
        p = g % 2
        xc, xck = xc_[p], f"xc{p}"
        if g + 1 < NTILE:
            nrows = slice((g + 1) * 512, (g + 2) * 512)
            s.dma("sp", xc_[(g + 1) % 2][:],
                  io["x1c_in"][nrows, :].rearrange("(c p) d -> p c d", p=128),
                  w=[f"xc{(g + 1) % 2}"])
        emit_ln(s, al, "ln_", xc[:], xck, xnfm[:], "xnfm", 4, pstA, "pstA", ident)
        for mc in range(8):
            pd = [psa, psb][mc % 2]
            for kc in range(8):
                s.mm(pd[:, 0:512], wkT[:, kc, 128 * mc:128 * (mc + 1)], xnfm[:, kc, :],
                     r=["wkT", "xnfm"], w=[pd.name], start=(kc == 0), stop=(kc == 7))
            s.cp("dve", kfm1[:, mc, :], pd[:, 0:512], r=[pd.name], w=["kfm1"])
        for tc in range(4):
            for half in range(2):
                pd = [psa, psb][half]
                for kc in range(8):
                    s.mm(pd[:, 0:512], xnfm[:, kc, 128 * tc:128 * (tc + 1)],
                         wvT[:, kc, 512 * half:512 * (half + 1)],
                         r=["wvT", "xnfm"], w=[pd.name], start=(kc == 0), stop=(kc == 7))
                s.cp("dve", vtm1[:, tc, 512 * half:512 * (half + 1)], pd[:, 0:512],
                     r=[pd.name], w=["vtm1"])
        for tc in range(4):
            def kv_consume(pr, kft, kftk, tc=tc):
                for par in range(2):
                    h = 2 * pr + par
                    for fc in range(2):
                        kvp = [kvp0, kvp1][fc]
                        s.mm(kvp[:, 64 * h:64 * h + 64],
                             kft[:, 256 * par + 128 * fc:256 * par + 128 * fc + 128],
                             vtm1[:, tc, 64 * h:64 * h + 64],
                             r=[kftk, "vtm1"], w=[kvp.name], skip=True)
                    for fc in range(2):
                        s.mm(ksp[:, 2 * h + fc:2 * h + fc + 1],
                             kft[:, 256 * par + 128 * fc:256 * par + 128 * fc + 128],
                             onesb[:, :],
                             r=[kftk, "onesb"], w=["ksp"], skip=True)
            pend = None
            for pr in range(8):
                pd = [psa, psb][pr % 2]
                for par in range(2):
                    h = 2 * pr + par
                    po = 64 * par
                    s.mm(pd[:, 256 * par:256 * par + 256],
                         kfm1[po:po + 64, pr, 128 * tc:128 * (tc + 1)],
                         wfd[po:po + 64, :],
                         r=["kfm1", "wfd"], w=[pd.name], tp=(po, 0), skip=True)
                pp = pr % 2
                exs, exsk = exs_[pp], f"exs{pp}"
                kft, kftk = kft_[pp], f"kft{pp}"
                s.ts("dve", kft[:], pd[:, 0:512], 0.0, None, Op.max, Op.bypass,
                     r=[pd.name], w=[kftk])
                s.act(exs[:], pd[:, 0:512], AF.Exp, r=[pd.name], w=[exsk])
                s.stt(kft[:], exs[:], 1.0, kft[:], Op.min, Op.add,
                      r=[exsk], w=[kftk])
                if pend is not None:
                    kv_consume(*pend)
                pend = (pr, kft, kftk)
            kv_consume(*pend)
            for fc, kvp in ((0, kvp0), (1, kvp1)):
                s.tt(kva[:, fc, :], kva[:, fc, :], kvp[:, 0:1024], Op.add,
                     r=["kva", kvp.name], w=["kva"])
            s.tt(ksa[:], ksa[:], ksp[:, 0:32], Op.add, r=["ksa", "ksp"], w=["ksa"])
    s.dma("pool", io["cc_in"][:, 0:1024], kva[:, 0, :], r=["kva"], w=["ccin_a"])
    s.dma("pool", io["cc_in"][:, 1024:2048], kva[:, 1, :], r=["kva"], w=["ccin_b"])
    s.dma("pool", io["cc_in"][:, 2048:2080], ksa[:], r=["ksa"], w=["ccin_c"])
    s.collective([io["cc_in"].opt()], [io["cc_out"].opt()],
                 r=["ccin_a", "ccin_b", "ccin_c"], w=["ccout"])
    s.dma("pool", kva[:, 0, :], io["cc_out"][:, 0:1024], r=["ccout"], w=["kva"])
    s.dma("pool", kva[:, 1, :], io["cc_out"][:, 1024:2048], r=["ccout"], w=["kva"])
    s.dma("pool", ksa[:], io["cc_out"][:, 2048:2080], r=["ccout"], w=["ksa"])
    s.dma("act", io["kvf"][0], kva[:, 0, :], r=["kva"], w=["kvfd"])
    s.dma("act", io["kvf"][1], kva[:, 1, :], r=["kva"], w=["kvfd2"])
    s.dma("act", io["ksumf"], ksa[:], r=["ksa"], w=["ksumfd"])


def emit_l1out(s, al, io):
    sb, ps = al.sb, al.ps
    ident = sb("ident", [128, 128], BF16)
    identf = sb("identf", [128, 128], F32)
    cvals = sb("cvals", [128, 4], F32)
    ebias = sb("ebias", [128, 16], F32)
    indn = sb("indn", [16, 1024], BF16)
    woT = sb("woT", [128, 8, 1024], BF16)
    wfd = sb("wfd", [128, 256], BF16)
    rtT = sb("rtT", [128, 8, 16], BF16)
    kvsb = sb("kvsb", [128, 2, 1024], BF16)
    kss = sb("kss", [128, 32], F32)
    knd = sb("knd", [128, 32, 16], BF16)
    xc = sb("xc", [128, 8, 1024], F32)
    xnfm = sb("xnfm", [128, 8, 1024], BF16)
    qfm1 = sb("qfm1", [128, 8, 512], BF16)
    qff_ = [sb(f"qff{i}", [128, 2, 512], BF16) for i in range(2)]
    exs_ = [sb(f"exs{i}", [128, 1024], BF16) for i in range(2)]
    nrc = sb("nrc", [16, 1024], BF16)
    rbb = sb("rbb", [128, 512], BF16)
    ofm = sb("ofm", [128, 8, 512], BF16)
    es1 = sb("es1", [16, 512], F32)
    sco = sb("sco", [16, 1024], F32)
    sctm = sb("sctm", [128, 8, 16], F32)
    scb = sb("scb", [128, 8, 16], F32)
    tmpc = sb("tmpc", [128, 8, 16], F32)
    cnt4 = sb("cnt4", [128, 8, 4], F32)
    gsum = sb("gsum", [128, 8, 4], F32)
    cntg = sb("cntg", [128, 8, 4], F32)
    gmask = sb("gmask", [128, 8, 4], F32)
    msc = sb("msc", [128, 8, 16], F32)
    cnt16 = sb("cnt16", [128, 8, 16], F32)
    sel4 = sb("sel4", [128, 8, 16], F32)
    cmb = sb("cmb", [128, 8, 16], F32)
    tsum = sb("tsum", [128, 8], F32)
    cmbF = sb("cmbF", [16, 1024], F32)
    cbr = sb("cbr", [128, 1024], BF16)
    sg = [sb(f"sg{i}", [128, 512], BF16) for i in range(2)]
    uc = [sb(f"uc{i}", [128, 512], BF16) for i in range(2)]
    hst = [sb(f"hst{u}", [128, 4, 1024], BF16) for u in range(3)]
    wga = [sb(f"wga{i}", [128, 8, 512], BF16) for i in range(2)]
    wua = [sb(f"wua{i}", [128, 8, 512], BF16) for i in range(2)]
    wda = [sb(f"wda{i}", [128, 4, 1024], BF16) for i in range(3)]
    _ln_scratch(al, "ln_")
    pg0 = ps("pg0", [128, 1024], F32)
    pg1 = ps("pg1", [128, 1024], F32)
    pya = ps("pya", [128, 1024], F32)
    pnn = ps("pnn", [16, 512], F32)
    pstA = ps("pstA", [128, 1024], BF16)

    s.dma("sp", ident[:], io["ident"], w=["ident"])
    s.dma("sp", identf[:], io["identf"], w=["identf"])
    s.dma("sp", cvals[:], io["cvals"], w=["cvals"])
    s.dma("sp", ebias[:], io["ebias"], w=["ebias"])
    s.dma("sp", indn[:], io["indn"], w=["indn"])
    s.dma("pool", woT[:], io["l1woT"].rearrange("(c p) m -> p c m", p=128), w=["woT"])
    s.dma("pool", wfd[:], io["wfTd"], w=["wfd"])
    s.dma("pool", rtT[:], io["routerT"].rearrange("(c p) m -> p c m", p=128), w=["rtT"])
    s.dma("pool", kvsb[:], io["kvf_in"].rearrange("a p m -> p a m"), w=["kvsb"])
    s.dma("sp", kss[:], io["ksumf_in"], w=["kss"])
    s.memset("dve", knd[:], 0.0, w=["knd"])
    for j in range(32):
        s.cp("dve", knd[:, j, j // 2:j // 2 + 1], kss[:, j:j + 1], r=["kss"], w=["knd"])

    GROUPS = [(0, 3), (3, 6), (6, 9), (9, 12), (12, 15), (15, 18)]
    for m in range(4):
        rows = slice(m * 1024, (m + 1) * 1024)
        s.dma("sp", xc[:], io["x1c_in"][rows, :].rearrange("(c p) d -> p c d", p=128),
              w=["xc"])
        wqr = io["l1wqT"].rearrange("(c p) m -> p c m", p=128)
        s.dma("pool", wga[0][:], wqr[:, :, 0:512], w=["wga0"])
        s.dma("pool", wga[1][:], wqr[:, :, 512:1024], w=["wga1"])
        emit_ln(s, al, "ln_", xc[:], "xc", xnfm[:], "xnfm", 8, pstA, "pstA", ident)

        for sub in range(2):
            tsl = slice(512 * sub, 512 * (sub + 1))
            for mc in range(8):
                pgx = [pg0, pg1][mc % 2]
                wqb = wga[mc // 4]
                for kc in range(8):
                    s.mm(pgx[:, 0:512], wqb[:, kc, 128 * (mc % 4):128 * (mc % 4 + 1)],
                         xnfm[:, kc, tsl], r=[wqb.name, "xnfm"], w=[pgx.name],
                         start=(kc == 0), stop=(kc == 7))
                s.cp("dve", qfm1[:, mc, :], pgx[:, 0:512], r=[pgx.name], w=["qfm1"])
            nfirst = True
            for pr in range(8):
                for par in range(2):
                    h = 2 * pr + par
                    po = 64 * par
                    pgx = [pg0, pg1][h % 2]
                    for fc in range(2):
                        s.mm(pgx[:, 512 * fc:512 * fc + 512],
                             wfd[po:po + 64, 128 * fc:128 * (fc + 1)],
                             qfm1[po:po + 64, pr, :],
                             r=["wfd", "qfm1"], w=[pgx.name], tp=(po, 0), skip=True)
                    pp = h % 2
                    exs, exsk = exs_[pp], f"exs{pp}"
                    qff, qffk = qff_[pp], f"qff{pp}"
                    qfl = qff[:].rearrange("p a t -> p (a t)")
                    s.ts("dve", qfl, pgx[:, 0:1024], 0.0, None, Op.max,
                         Op.bypass, r=[pgx.name], w=[qffk])
                    s.act(exs[:], pgx[:, 0:1024], AF.Exp, r=[pgx.name], w=[exsk])
                    s.op("dve", lambda e, qfl=qfl, exs=exs: e.scalar_tensor_tensor(
                        qfl, exs[:], 1.0, qfl,
                        Op.min, Op.add), r=[exsk], w=[qffk])
                    for fc in range(2):
                        s.mm(pya[po:po + 64, 0:512], kvsb[:, fc, 64 * h:64 * h + 64],
                             qff[:, fc, :], r=["kvsb", qffk], w=["pya"],
                             start=(fc == 0), stop=(fc == 1), skip=True, tp=(0, po))
                        s.mm(pnn[0:16, 0:512], knd[:, 2 * h + fc, :], qff[:, fc, :],
                             r=["knd", qffk], w=["pnn"], start=nfirst, stop=False,
                             skip=True)
                        nfirst = False
                s.cp("act", ofm[:, pr, :], pya[:, 0:512], r=["pya"], w=["ofm"])
            s.act(es1[:], pnn[0:16, 0:512], AF.Ln, r=["pnn", "cvals"], w=["es1"],
                  bias=cvals[0:16, 2:3])
            s.act(nrc[:, tsl], es1[:], AF.Exp, r=["es1"], w=["nrc"], scale=-1.0)
            for pr in range(8):
                s.mm(pya[:, 0:512], indn[:, 128 * pr:128 * (pr + 1)], nrc[:, tsl],
                     r=["indn", "nrc"], w=["pya"])
                s.cp("act", rbb[:], pya[:, 0:512], r=["pya"], w=["rbb"])
                s.tt(ofm[:, pr, :], ofm[:, pr, :], rbb[:], Op.mult,
                     r=["ofm", "rbb"], w=["ofm"])
            for tc in range(4):
                mc = sub * 4 + tc
                for half in range(2):
                    hs = slice(512 * half, 512 * (half + 1))
                    for pr in range(8):
                        s.mm(pya[:, hs], ofm[:, pr, 128 * tc:128 * (tc + 1)],
                             woT[:, pr, hs], r=["ofm", "woT"], w=["pya"],
                             start=(pr == 0), stop=(pr == 7))
                s.tt(xc[:, mc, :], xc[:, mc, :], pya[:, 0:1024], Op.add,
                     r=["xc", "pya"], w=["xc"])

        for half in range(2):
            for kc in range(8):
                s.mm(pnn[0:16, 0:512], rtT[:, kc, :], xnfm[:, kc, 512 * half:512 * (half + 1)],
                     r=["rtT", "xnfm"], w=["pnn"], start=(kc == 0), stop=(kc == 7))
            s.act(es1[:], pnn[0:16, 0:512], AF.Exp, r=["pnn"], w=["es1"], scale=-1.0)
            s.ts("dve", es1[:], es1[:], 1.0, None, Op.add, Op.bypass, r=["es1"], w=["es1"])
            s.recip(sco[:, 512 * half:512 * (half + 1)], es1[:], r=["es1"], w=["sco"])
        for tc in range(8):
            s.trans(pya[:, 0:16], sco[0:16, 128 * tc:128 * (tc + 1)], identf[0:16, 0:16],
                    r=["sco", "identf"], w=["pya"])
            s.cp("dve", sctm[:, tc, :], pya[:, 0:16], r=["pya"], w=["sctm"])
        s.op("dve", lambda e: e.tensor_tensor(
            scb[:], sctm[:], ebias[:].unsqueeze(1).broadcast_to([128, 8, 16]), Op.add),
            r=["sctm", "ebias"], w=["scb"])

        def ranks(src_ap, srckey, cnt_ap, cntkey, n, viewfn, tmpfn):
            s.memset("dve", cnt_ap, 0.0, w=[cntkey])
            for d in range(1, n):
                L = viewfn(src_ap, 0, n - d)
                Rv = viewfn(src_ap, d, n - d)
                cL = viewfn(cnt_ap, 0, n - d)
                cR = viewfn(cnt_ap, d, n - d)
                tv = tmpfn(n - d)
                s.tt(tv, L, Rv, Op.is_ge, r=[srckey], w=["tmpc"])
                s.tt(cR, cR, tv, Op.add, r=[cntkey, "tmpc"], w=[cntkey])
                s.tt(tv, Rv, L, Op.is_gt, r=[srckey], w=["tmpc"])
                s.tt(cL, cL, tv, Op.add, r=[cntkey, "tmpc"], w=[cntkey])

        def v16(ap, off, ln):
            return ap[:, :, off:off + ln]

        def v4(ap, off, ln):
            return ap.rearrange("p a (b c) -> p (a b) c", c=4)[:, :, off:off + ln]

        def v4g(ap, off, ln):
            return ap[:, :, off:off + ln]

        ranks(scb[:], "scb", cnt16[:], "cnt16", 4, v4,
              lambda ln: tmpc[:].rearrange("p a (b c) -> p (a b) c", c=4)[:, :, 0:ln])
        s.ts("dve", cnt16[:], cnt16[:], 2.0, None, Op.is_lt, Op.bypass,
             r=["cnt16"], w=["cnt16"])
        s.tt(msc[:], scb[:], cnt16[:], Op.mult, r=["scb", "cnt16"], w=["msc"])
        s.op("dve", lambda e: e.tensor_reduce(
            gsum[:], msc[:].rearrange("p a (b c) -> p (a b) c", c=4),
            mybir.AxisListType.X, Op.add), r=["msc"], w=["gsum"])
        ranks(gsum[:], "gsum", cntg[:], "cntg", 4, v4g,
              lambda ln: tmpc[:, :, 0:ln])
        s.ts("dve", gmask[:], cntg[:], 2.0, None, Op.is_lt, Op.bypass,
             r=["cntg"], w=["gmask"])
        s.op("dve", lambda e: e.tensor_tensor(
            msc[:].rearrange("p a (b c) -> p a b c", c=4),
            scb[:].rearrange("p a (b c) -> p a b c", c=4),
            gmask[:].unsqueeze(3).broadcast_to([128, 8, 4, 4]), Op.mult),
            r=["scb", "gmask"], w=["msc"])
        ranks(msc[:], "msc", cnt16[:], "cnt16", 16, v16,
              lambda ln: tmpc[:, :, 0:ln])
        s.ts("dve", sel4[:], cnt16[:], 4.0, None, Op.is_lt, Op.bypass,
             r=["cnt16"], w=["sel4"])
        s.tt(cmb[:], sctm[:], sel4[:], Op.mult, r=["sctm", "sel4"], w=["cmb"])
        s.op("dve", lambda e: e.tensor_reduce(
            tsum[:], cmb[:], mybir.AxisListType.X, Op.add), r=["cmb"], w=["tsum"])
        s.ts("dve", tsum[:], tsum[:], 1e-20, None, Op.add, Op.bypass,
             r=["tsum"], w=["tsum"])
        s.recip(tsum[:], tsum[:], r=["tsum"], w=["tsum"])
        for mc in range(8):
            s.ts("dve", cmb[:, mc, :], cmb[:, mc, :], tsum[:, mc:mc + 1], 2.5,
                 Op.mult, Op.mult, r=["cmb", "tsum"], w=["cmb"])
        for tc in range(8):
            s.trans(pya[0:16, 0:128], cmb[:, tc, :], identf[:],
                    r=["cmb", "identf"], w=["pya"])
            s.cp("dve", cmbF[:, 128 * tc:128 * (tc + 1)], pya[0:16, 0:128],
                 r=["pya"], w=["cmbF"])
        s.dma("act", io["cmbd"], cmbF[:], r=["cmbF"], w=["cmbd"])

        for gi, (e0, e1) in enumerate(GROUPS):
            for e in range(e0, e1):
                u = e - e0
                gb, ub, db = wga[e % 2], wua[e % 2], wda[u]
                s.dma("pool", gb[:], io["gateT"][e].rearrange("(c p) m -> p c m", p=128),
                      w=[gb.name])
                s.dma("pool", ub[:], io["upT"][e].rearrange("(c p) m -> p c m", p=128),
                      w=[ub.name])
                s.dma("pool", db[:], io["downT"][e].rearrange("(c p) m -> p c m", p=128),
                      w=[db.name])
                if e < 16:
                    s.dma("pool", cbr[:], io["cmbd"][e].unsqueeze(0).broadcast_to([128, 1024]),
                          r=["cmbd"], w=["cbr"])
                for ic in range(4):
                    for sub in range(2):
                        tsl = slice(512 * sub, 512 * (sub + 1))
                        pp = (ic * 2 + sub) % 2
                        pgx = [pg0, pg1][pp]
                        for kc in range(8):
                            s.mm(pgx[:, 0:512], gb[:, kc, 128 * ic:128 * (ic + 1)],
                                 xnfm[:, kc, tsl], r=[gb.name, "xnfm"], w=[pgx.name],
                                 start=(kc == 0), stop=(kc == 7), skip=True)
                            s.mm(pgx[:, 512:1024], ub[:, kc, 128 * ic:128 * (ic + 1)],
                                 xnfm[:, kc, tsl], r=[ub.name, "xnfm"], w=[pgx.name],
                                 start=(kc == 0), stop=(kc == 7), skip=True)
                        s.act(sg[pp][:], pgx[:, 0:512], AF.Silu, r=[pgx.name], w=[f"sg{pp}"])
                        if e < 16:
                            s.tt(uc[pp][:], pgx[:, 512:1024], cbr[:, tsl], Op.mult,
                                 r=[pgx.name, "cbr"], w=[f"uc{pp}"])
                        else:
                            s.cp("dve", uc[pp][:], pgx[:, 512:1024], r=[pgx.name], w=[f"uc{pp}"])
                        s.tt(hst[u][:, ic, tsl], sg[pp][:], uc[pp][:], Op.mult,
                             r=[f"sg{pp}", f"uc{pp}"], w=[f"hst{u}"])
            for tc in range(8):
                nu = e1 - e0
                for half in range(2):
                    hs = slice(512 * half, 512 * (half + 1))
                    kk = 0
                    for u in range(nu):
                        for ic in range(4):
                            s.mm(pya[:, hs], hst[u][:, ic, 128 * tc:128 * (tc + 1)],
                                 wda[u][:, ic, hs], r=[f"hst{u}", wda[u].name],
                                 w=["pya"], start=(kk == 0), stop=(kk == 4 * nu - 1))
                            kk += 1
                s.tt(xc[:, tc, :], xc[:, tc, :], pya[:, 0:1024], Op.add,
                     r=["xc", "pya"], w=["xc"])
        s.dma("act", io["outp"][rows, :].rearrange("(c p) d -> p c d", p=128), xc[:],
              r=["xc"], w=["outd"])


def _dram(nc, name, shape, dt, kind):
    return nc.dram_tensor("d_" + name, shape, dt, kind=kind).ap()


def build_l0():
    nc = bass.Bass("TRN2", target_bir_lowering=False)
    io = {"x": _dram(nc, "x", [TPC, D], F32, "ExternalInput")}
    for nm, shp in (("Wsel", [D, 160]), ("dtW", [32, 512]), ("Wk", [D, 512]),
                    ("Wv", [D, 512]), ("Wr", [D, 512]), ("Wg", [D, 512]),
                    ("Wo", [512, D]), ("rhsT", [D, D])):
        io[nm] = _dram(nc, nm, shp, F32, "ExternalInput")
    for nm, shp, dt in (("ident", [128, 128], BF16), ("amask", [128, 1024], U8),
                        ("onesrow", [1, 128], F32), ("onesb", [128, 1], BF16),
                        ("cvals", [128, 4], F32),
                        ("dtbb", [128, 512], F32), ("bdb", [128, 512], F32),
                        ("tfc", [128, 4], F32), ("send_mask", [128, 1], F32)):
        io[nm] = _dram(nc, nm, shp, dt, "ExternalInput")
    for nm in ("tri16", "tri32", "tri64", "tri128", "rev16", "rev32", "rev64", "rev128"):
        io[nm] = _dram(nc, nm, [128, 128], F32, "ExternalInput")
    io["x1"] = _dram(nc, "x1", [TPC, D], F32, "ExternalOutput")
    io["qglob"] = _dram(nc, "qglob", [NCHUNK, 128, 512], BF16, "ExternalOutput")
    io["srg"] = _dram(nc, "srg", [NTILE, 128, 4, 512], BF16, "ExternalOutput")
    io["st_mid"] = _dram(nc, "st_mid", [128, 256], F32, "ExternalOutput")
    io["scr8"] = _dram(nc, "scr8", [8, 128], F32, "Internal")
    io["cc_in"] = nc.dram_tensor("d_cc_in", [128, 256], F32).ap()
    io["cc_out"] = nc.dram_tensor("d_cc_out", [128, 256], F32).ap()
    es = contextlib.ExitStack()
    with es:
        sch = Sched(nc)
        al = Alloc(nc, es, sch)
        emit_l0(sch, al, io)
        sch.finalize(es)
    return nc


def build_l0c():
    nc = bass.Bass("TRN2", target_bir_lowering=False)
    io = {
        "x1_in": _dram(nc, "x1_in", [TPC, D], F32, "ExternalInput"),
        "qglob_in": _dram(nc, "qglob_in", [NCHUNK, 128, 512], BF16, "ExternalInput"),
        "srg_in": _dram(nc, "srg_in", [NTILE, 128, 4, 512], BF16, "ExternalInput"),
        "st_mid_in": _dram(nc, "st_mid_in", [128, 256], F32, "ExternalInput"),
        "use_mask": _dram(nc, "use_mask", [128, 1], F32, "ExternalInput"),
        "Wo": _dram(nc, "Wo", [512, D], F32, "ExternalInput"),
        "x1c": _dram(nc, "x1c", [TPC, D], F32, "ExternalOutput"),
    }
    es = contextlib.ExitStack()
    with es:
        sch = Sched(nc)
        al = Alloc(nc, es, sch)
        emit_l0c(sch, al, io)
        sch.finalize(es)
    return nc


def build_l1kv():
    nc = bass.Bass("TRN2", target_bir_lowering=False)
    io = {
        "x1c_in": _dram(nc, "x1c_in", [TPC, D], F32, "ExternalInput"),
        "l1wkT": _dram(nc, "l1wkT", [D, D], F32, "ExternalInput"),
        "l1wvT": _dram(nc, "l1wvT", [D, D], F32, "ExternalInput"),
        "wfTd": _dram(nc, "wfTd", [128, 256], F32, "ExternalInput"),
        "ident": _dram(nc, "ident", [128, 128], BF16, "ExternalInput"),
        "cvals": _dram(nc, "cvals", [128, 4], F32, "ExternalInput"),
        "onesb": _dram(nc, "onesb", [128, 1], BF16, "ExternalInput"),
        "kvf": _dram(nc, "kvf", [2, 128, 1024], F32, "ExternalOutput"),
        "ksumf": _dram(nc, "ksumf", [128, 32], F32, "ExternalOutput"),
    }
    io["cc_in"] = nc.dram_tensor("d_cc_in", [128, 2080], F32).ap()
    io["cc_out"] = nc.dram_tensor("d_cc_out", [128, 2080], F32).ap()
    es = contextlib.ExitStack()
    with es:
        sch = Sched(nc)
        al = Alloc(nc, es, sch)
        emit_l1kv(sch, al, io)
        sch.finalize(es)
    return nc


def build_l1out():
    nc = bass.Bass("TRN2", target_bir_lowering=False)
    io = {
        "x1c_in": _dram(nc, "x1c_in", [TPC, D], F32, "ExternalInput"),
        "kvf_in": _dram(nc, "kvf_in", [2, 128, 1024], F32, "ExternalInput"),
        "ksumf_in": _dram(nc, "ksumf_in", [128, 32], F32, "ExternalInput"),
        "l1wqT": _dram(nc, "l1wqT", [D, D], F32, "ExternalInput"),
        "l1woT": _dram(nc, "l1woT", [D, D], F32, "ExternalInput"),
        "wfTd": _dram(nc, "wfTd", [128, 256], F32, "ExternalInput"),
        "routerT": _dram(nc, "routerT", [D, 16], F32, "ExternalInput"),
        "ebias": _dram(nc, "ebias", [128, 16], F32, "ExternalInput"),
        "indn": _dram(nc, "indn", [16, 1024], BF16, "ExternalInput"),
        "ident": _dram(nc, "ident", [128, 128], BF16, "ExternalInput"),
        "identf": _dram(nc, "identf", [128, 128], F32, "ExternalInput"),
        "cvals": _dram(nc, "cvals", [128, 4], F32, "ExternalInput"),
        "gateT": _dram(nc, "gateT", [18, D, 512], F32, "ExternalInput"),
        "upT": _dram(nc, "upT", [18, D, 512], F32, "ExternalInput"),
        "downT": _dram(nc, "downT", [18, 512, D], F32, "ExternalInput"),
        "cmbd": _dram(nc, "cmbd", [16, 1024], F32, "Internal"),
        "outp": _dram(nc, "outp", [TPC, D], F32, "ExternalOutput"),
    }
    es = contextlib.ExitStack()
    with es:
        sch = Sched(nc)
        al = Alloc(nc, es, sch)
        emit_l1out(sch, al, io)
        sch.finalize(es)
    return nc


def build_fused():
    nc = bass.Bass("TRN2", target_bir_lowering=False)
    io = {}
    io["x"] = _dram(nc, "x", [TPC, D], F32, "ExternalInput")
    for nm, shp in (("Wsel", [D, 160]), ("dtW", [32, 512]), ("Wk", [D, 512]),
                    ("Wv", [D, 512]), ("Wr", [D, 512]), ("Wg", [D, 512]),
                    ("Wo", [512, D]), ("rhsT", [D, D]),
                    ("l1wkT", [D, D]), ("l1wvT", [D, D]), ("l1wqT", [D, D]),
                    ("l1woT", [D, D]), ("routerT", [D, 16]),
                    ("gateT", [18, D, 512]), ("upT", [18, D, 512]),
                    ("downT", [18, 512, D]), ("wfTd", [128, 256])):
        io[nm] = _dram(nc, nm, shp, F32, "ExternalInput")
    for nm, shp, dt in (("ident", [128, 128], BF16), ("identf", [128, 128], F32),
                        ("amask", [128, 1024], U8),
                        ("onesrow", [1, 128], F32), ("onesb", [128, 1], BF16),
                        ("cvals", [128, 4], F32),
                        ("dtbb", [128, 512], F32), ("bdb", [128, 512], F32),
                        ("tfc", [128, 4], F32), ("send_mask", [128, 1], F32),
                        ("use_mask", [128, 1], F32), ("ebias", [128, 16], F32),
                        ("indn", [16, 1024], BF16)):
        io[nm] = _dram(nc, nm, shp, dt, "ExternalInput")
    for nm in ("tri16", "tri32", "tri64", "tri128", "rev16", "rev32", "rev64", "rev128"):
        io[nm] = _dram(nc, nm, [128, 128], F32, "ExternalInput")
    io["outp"] = _dram(nc, "outp", [TPC, D], F32, "ExternalOutput")
    for nm, shp, dt in (("x1", [TPC, D], F32), ("qglob", [NCHUNK, 128, 512], BF16),
                        ("srg", [NTILE, 128, 4, 512], BF16), ("st_mid", [128, 256], F32),
                        ("scr8", [8, 128], F32), ("x1c", [TPC, D], F32),
                        ("kvf", [2, 128, 1024], F32), ("ksumf", [128, 32], F32),
                        ("cmbd", [16, 1024], F32)):
        io[nm] = _dram(nc, nm, shp, dt, "Internal")
    io["cc_st_in"] = nc.dram_tensor("d_cc_st_in", [128, 256], F32).ap()
    io["cc_st_out"] = nc.dram_tensor("d_cc_st_out", [128, 256], F32).ap()
    io["cc_kv_in"] = nc.dram_tensor("d_cc_kv_in", [128, 2080], F32).ap()
    io["cc_kv_out"] = nc.dram_tensor("d_cc_kv_out", [128, 2080], F32).ap()

    io0 = dict(io)
    io0["cc_in"], io0["cc_out"] = io["cc_st_in"], io["cc_st_out"]
    io0c = dict(io, x1_in=io["x1"], qglob_in=io["qglob"], srg_in=io["srg"],
                st_mid_in=io["st_mid"])
    io1k = dict(io, x1c_in=io["x1c"])
    io1k["cc_in"], io1k["cc_out"] = io["cc_kv_in"], io["cc_kv_out"]
    io1o = dict(io, x1c_in=io["x1c"], kvf_in=io["kvf"], ksumf_in=io["ksumf"])

    es_outer = contextlib.ExitStack()
    with es_outer:
        sch = Sched(nc)
        phases = ((emit_l0, io0, "p0_"), (emit_l0c, io0c, "p1_"),
                  (emit_l1kv, io1k, "p2_"), (emit_l1out, io1o, "p3_"))
        for emit, iom, pref in phases:
            with contextlib.ExitStack() as es_p:
                al = Alloc(nc, es_p, sch, prefix=pref)
                sch.ns = pref
                emit(sch, al, iom)
            sch.barrier()
        sch.ns = ""
        sch.finalize(es_outer)
    return nc


def kernel_fused(ins, consts, wprep):
    x = ins["L0_attn_x"]
    nc = build_fused()
    base = {
        "Wsel": ins["L0_attn_W_sel"], "dtW": ins["L0_attn_dt_W"],
        "Wk": ins["L0_attn_Wk"], "Wv": ins["L0_attn_Wv"], "Wr": ins["L0_attn_Wr"],
        "Wg": ins["L0_attn_Wg"], "Wo": ins["L0_attn_Wo"],
        "rhsT": np.ascontiguousarray(ins["L0_ffn_rhs"].T),
    }
    for nm in ("l1wqT", "l1wkT", "l1wvT", "l1woT", "routerT", "gateT", "upT", "downT"):
        base[nm] = wprep[nm]
    for nm in ("ident", "identf", "amask", "onesrow", "onesb", "cvals", "dtbb",
               "bdb", "tfc", "ebias", "indn", "wfTd",
               "tri16", "tri32", "tri64", "tri128", "rev16", "rev32", "rev64", "rev128"):
        base[nm] = consts[nm]
    in_maps = []
    for c in range(8):
        m = dict(base)
        b, h = c // 2, c % 2
        m["x"] = np.ascontiguousarray(x[b, h * TPC:(h + 1) * TPC, :])
        m["send_mask"] = np.full((128, 1), 1.0 if h == 0 else 0.0, np.float32)
        m["use_mask"] = np.full((128, 1), 0.0 if h == 0 else 1.0, np.float32)
        in_maps.append({"d_" + k: v for k, v in m.items()})
    res = run_bass_kernel_spmd(nc, in_maps, core_ids=list(range(8)))
    out = np.zeros((B, L, D), np.float32)
    for c in range(8):
        b, h = c // 2, c % 2
        out[b, h * TPC:(h + 1) * TPC, :] = res.results[c]["d_outp"]
    return out


def _prep_weights(ins):
    w = {}
    w["l1wqT"] = np.ascontiguousarray(ins["L1_attn_Wq"].T)
    w["l1wkT"] = np.ascontiguousarray(ins["L1_attn_Wk"].T)
    w["l1wvT"] = np.ascontiguousarray(ins["L1_attn_Wv"].T)
    w["l1woT"] = np.ascontiguousarray(ins["L1_attn_Wo"].T)
    w["routerT"] = np.ascontiguousarray(ins["L1_ffn_router_weight"].T)
    gT = ins["L1_ffn_gate_w"].transpose(0, 2, 1)
    shg = ins["L1_ffn_sh_gate_w"].T.reshape(D, 2, 512).transpose(1, 0, 2)
    w["gateT"] = np.ascontiguousarray(np.concatenate([gT, shg], 0))
    uT = ins["L1_ffn_up_w"].transpose(0, 2, 1)
    shu = ins["L1_ffn_sh_up_w"].T.reshape(D, 2, 512).transpose(1, 0, 2)
    w["upT"] = np.ascontiguousarray(np.concatenate([uT, shu], 0))
    dT = ins["L1_ffn_down_w"].transpose(0, 2, 1)
    shd = ins["L1_ffn_sh_down_w"].T.reshape(2, 512, D)
    w["downT"] = np.ascontiguousarray(np.concatenate([dT, shd], 0))
    return w


def kernel(**inputs):
    ins = {k: np.asarray(v) for k, v in inputs.items()}
    consts = host_constants(ins)
    wprep = _prep_weights(ins)
    if MODE == "fused":
        return kernel_fused(ins, consts, wprep)
    x = ins["L0_attn_x"]

    nc0 = build_l0()
    base0 = {
        "Wsel": ins["L0_attn_W_sel"], "dtW": ins["L0_attn_dt_W"],
        "Wk": ins["L0_attn_Wk"], "Wv": ins["L0_attn_Wv"], "Wr": ins["L0_attn_Wr"],
        "Wg": ins["L0_attn_Wg"], "Wo": ins["L0_attn_Wo"],
        "rhsT": np.ascontiguousarray(ins["L0_ffn_rhs"].T),
    }
    for nm in ("ident", "amask", "onesrow", "onesb", "cvals", "dtbb", "bdb", "tfc",
               "tri16", "tri32", "tri64", "tri128", "rev16", "rev32", "rev64", "rev128"):
        base0[nm] = consts[nm]
    in_maps = []
    for c in range(8):
        m = dict(base0)
        b, h = c // 2, c % 2
        m["x"] = np.ascontiguousarray(x[b, h * TPC:(h + 1) * TPC, :])
        m["send_mask"] = np.full((128, 1), 1.0 if h == 0 else 0.0, np.float32)
        in_maps.append({"d_" + k: v for k, v in m.items()})
    res0 = run_bass_kernel_spmd(nc0, in_maps, core_ids=list(range(8)))

    nc1 = build_l0c()
    in_maps1 = []
    for c in range(8):
        r = res0.results[c]
        in_maps1.append({"d_" + k: v for k, v in {
            "x1_in": r["d_x1"], "qglob_in": r["d_qglob"], "srg_in": r["d_srg"],
            "st_mid_in": r["d_st_mid"],
            "use_mask": np.full((128, 1), 0.0 if c % 2 == 0 else 1.0, np.float32),
            "Wo": ins["L0_attn_Wo"],
        }.items()})
    res1 = run_bass_kernel_spmd(nc1, in_maps1, core_ids=list(range(8)))

    nc2 = build_l1kv()
    in_maps2 = []
    for c in range(8):
        in_maps2.append({"d_" + k: v for k, v in {
            "x1c_in": res1.results[c]["d_x1c"],
            "l1wkT": wprep["l1wkT"], "l1wvT": wprep["l1wvT"],
            "wfTd": consts["wfTd"], "ident": consts["ident"],
            "cvals": consts["cvals"], "onesb": consts["onesb"],
        }.items()})
    res2 = run_bass_kernel_spmd(nc2, in_maps2, core_ids=list(range(8)))

    nc3 = build_l1out()
    in_maps3 = []
    for c in range(8):
        in_maps3.append({"d_" + k: v for k, v in {
            "x1c_in": res1.results[c]["d_x1c"],
            "kvf_in": res2.results[c]["d_kvf"], "ksumf_in": res2.results[c]["d_ksumf"],
            "l1wqT": wprep["l1wqT"], "l1woT": wprep["l1woT"],
            "wfTd": consts["wfTd"], "routerT": wprep["routerT"],
            "ebias": consts["ebias"], "indn": consts["indn"],
            "ident": consts["ident"], "identf": consts["identf"],
            "cvals": consts["cvals"],
            "gateT": wprep["gateT"], "upT": wprep["upT"], "downT": wprep["downT"],
        }.items()})
    res3 = run_bass_kernel_spmd(nc3, in_maps3, core_ids=list(range(8)))

    out = np.zeros((B, L, D), np.float32)
    for c in range(8):
        b, h = c // 2, c % 2
        out[b, h * TPC:(h + 1) * TPC, :] = res3.results[c]["d_outp"]
    return out
